# version 11
# baseline (speedup 1.0000x reference)
import os
import numpy as np
import ml_dtypes
import concourse.bass as bass
import concourse.mybir as mybir
from concourse.alu_op_type import AluOpType
from concourse.bass_utils import run_bass_kernel_spmd
from contextlib import ExitStack
from functools import partial

F32 = mybir.dt.float32
BF16 = mybir.dt.bfloat16
I32 = mybir.dt.int32
AF = mybir.ActivationFunctionType
OP = AluOpType
AXX = mybir.AxisListType.X

H = 4096
QT = 1024
FC = 32
HEADS = 32
DFF = 11008
RC = 86
CC_QKV = 96
SEQ = 4096
KC = 32
NRANK = 4
SCALE = 1.0 / np.sqrt(128.0)
EPS = 1e-5

ENGS = ("sync", "gpsimd", "vector", "scalar", "tensor")
FFN_PASSES = [(0, 29), (29, 58), (58, 86)]
STAGE_FULL = 9

SEM_NAMES = ("ld", "lx0", "lx1", "lw0", "lw1", "lw2", "lw3", "lk0", "lk1",
             "lv0", "lv1", "ls0", "ls1", "ly0", "ly1", "gg0", "gg1",
             "gd0", "gd1", "cc", "pe", "act", "dve", "pool")


class Sch:
    def __init__(self, cnt):
        self.ops = {e: [] for e in ENGS}
        self.cnt = cnt

    SELF_SEM = ({"vector": "dve", "scalar": "act", "gpsimd": "pool"}
                if os.environ.get("K_SELFSYNC") else {"gpsimd": "pool"})

    def emit(self, eng, fn=None, waits=(), inc=None, amt=1):
        waits = list(waits)
        ss = self.SELF_SEM.get(eng)
        if ss is not None and fn is not None and self.cnt.get(ss, 0) > 0:
            waits.append((ss, self.cnt[ss]))
        incs = []
        if inc is not None:
            self.cnt[inc] = self.cnt.get(inc, 0) + amt
            incs = [(inc, amt)]
        self.ops[eng].append((tuple(waits), incs, fn))
        return self.cnt.get(inc, 0)

    def at(self, sem):
        return self.cnt.get(sem, 0)


def run_phase(nc, sems, sch):
    with nc.Block() as block:
        for eng in ENGS:
            ops = sch.ops[eng]
            if not ops:
                continue

            def body(h, _ops=ops):
                waited = {}
                for waits, incs, fn in _ops:
                    for (s, v) in waits:
                        if v > 0 and waited.get(s, -1) < v:
                            h.wait_ge(sems[s], v)
                            waited[s] = v
                    inst = fn(h) if fn is not None else None
                    for (s, a) in incs:
                        inst.then_inc(sems[s], a)
            getattr(block, eng)(body)


def bwaits(cnt):
    return tuple((s, v) for s, v in cnt.items() if v > 0)


def _build(stage=STAGE_FULL):
    nc = bass.Bass("TRN2", target_bir_lowering=False)

    d_x = nc.dram_tensor("x", [H, QT], F32, kind="ExternalInput").ap()
    d_qkvw = nc.dram_tensor("qkvw", [CC_QKV, 128, FC, 128], F32, kind="ExternalInput").ap()
    d_qkvb = nc.dram_tensor("qkvb", [128, CC_QKV], F32, kind="ExternalInput").ap()
    d_ow = nc.dram_tensor("ow", [FC, 128, FC, 128], F32, kind="ExternalInput").ap()
    d_ob = nc.dram_tensor("ob", [128, FC], F32, kind="ExternalInput").ap()
    d_gw = nc.dram_tensor("gw", [RC, 128, FC, 128], I32, kind="ExternalInput").ap()
    d_uw = nc.dram_tensor("uw", [RC, 128, FC, 128], I32, kind="ExternalInput").ap()
    d_dw = nc.dram_tensor("dw", [FC, 128, RC, 128], I32, kind="ExternalInput").ap()
    d_gs = nc.dram_tensor("gs", [128, RC], F32, kind="ExternalInput").ap()
    d_us = nc.dram_tensor("us", [128, RC], F32, kind="ExternalInput").ap()
    d_ds = nc.dram_tensor("ds", [128, FC], F32, kind="ExternalInput").ap()
    d_ones = nc.dram_tensor("ones_bf", [128, 128], BF16, kind="ExternalInput").ap()
    d_ident = nc.dram_tensor("ident_f32", [128, 128], F32, kind="ExternalInput").ap()
    d_eps = nc.dram_tensor("eps_col", [128, 2], F32, kind="ExternalInput").ap()

    d_y = nc.dram_tensor("y", [H, QT], F32, kind="ExternalOutput").ap()

    d_k_loc = nc.dram_tensor("k_loc", [H, QT], BF16)
    d_v_loc = nc.dram_tensor("v_loc", [H, QT], BF16)
    d_k_full = nc.dram_tensor("k_full", [NRANK * H, QT], BF16)
    d_v_full = nc.dram_tensor("v_full", [NRANK * H, QT], BF16)
    d_x2 = nc.dram_tensor("x2d", [H, QT], F32)
    d_x3p = nc.dram_tensor("x3p", [H, QT], F32)

    dbg = {}
    if stage < STAGE_FULL:
        dbg["xln"] = nc.dram_tensor("dbg_xln", [H, QT], BF16, kind="ExternalOutput").ap()
        dbg["q"] = nc.dram_tensor("dbg_q", [H, QT], BF16, kind="ExternalOutput").ap()
        dbg["kv"] = nc.dram_tensor("dbg_kv", [2 * H, QT], BF16, kind="ExternalOutput").ap()
        dbg["ctx"] = nc.dram_tensor("dbg_ctx", [H, QT], BF16, kind="ExternalOutput").ap()
        dbg["x2"] = nc.dram_tensor("dbg_x2", [H, QT], F32, kind="ExternalOutput").ap()
        dbg["xln2"] = nc.dram_tensor("dbg_xln2", [H, QT], BF16, kind="ExternalOutput").ap()

    cnt = {}
    sems = {}
    _SUB = int(os.environ.get("K_SUBSTAGE", "99"))
    _ASUB = int(os.environ.get("K_ASUB", "99"))

    def dma(h, out, in_):
        return h.dma_start(out=out, in_=in_)

    with ExitStack() as top:
        e = top.enter_context
        for s in SEM_NAMES:
            sems[s] = e(nc.semaphore(s))

        c_qkvb = e(nc.sbuf_tensor("c_qkvb", [128, CC_QKV], F32))
        c_ob = e(nc.sbuf_tensor("c_ob", [128, FC], F32))
        c_gs = e(nc.sbuf_tensor("c_gs", [128, RC], F32))
        c_us = e(nc.sbuf_tensor("c_us", [128, RC], F32))
        c_ds = e(nc.sbuf_tensor("c_ds", [128, FC], F32))
        c_ones = e(nc.sbuf_tensor("c_ones", [128, 128], BF16))
        c_ident = e(nc.sbuf_tensor("c_ident", [128, 128], F32))
        c_eps = e(nc.sbuf_tensor("c_eps", [128, 2], F32))
        t_mhq = e(nc.sbuf_tensor("t_mhq", [128, HEADS], F32))
        t_negm = e(nc.sbuf_tensor("t_negm", [128, 2], F32))
        t_sc1 = e(nc.sbuf_tensor("t_sc1", [128, 2], F32))

        sch = Sch(cnt)
        for dst, src in ((c_qkvb, d_qkvb), (c_ob, d_ob), (c_gs, d_gs), (c_us, d_us),
                         (c_ds, d_ds), (c_ones, d_ones), (c_ident, d_ident), (c_eps, d_eps)):
            sch.emit("sync", partial(lambda h, d_, s_: dma(h, d_[:], s_), d_=dst, s_=src),
                     inc="ld", amt=16)
        run_phase(nc, sems, sch)

        gstage = e(nc.sbuf_tensor("gstage", [128, 2 * FC * 128], BF16))
        ustage = e(nc.sbuf_tensor("ustage", [128, 2 * FC * 128], BF16))
        qctx_cm = nc.sbuf_tensor("qctx", [128, HEADS * QT], BF16)
        qctx = qctx_cm.__enter__()
        qctx_open = [True]

        def close_qctx():
            if qctx_open[0]:
                qctx_open[0] = False
                qctx_cm.__exit__(None, None, None)
        try:

                with ExitStack() as ctxB:
                    eB = ctxB.enter_context
                    xln = eB(nc.sbuf_tensor("xln", [128, FC * QT], BF16))
                    xc = eB(nc.sbuf_tensor("xc", [128, 2 * QT], F32))
                    xsq = eB(nc.sbuf_tensor("xsq", [128, QT], BF16))
                    tmp = eB(nc.sbuf_tensor("tmpf", [128, QT], F32))
                    wbf = eB(nc.sbuf_tensor("wbf", [128, 3 * 4096], BF16))
                    kevac = eB(nc.sbuf_tensor("kevac", [128, 2 * QT], BF16))

                    pSum1 = eB(nc.psum_tensor("pSum1", [128, QT], F32))
                    pSum2 = eB(nc.psum_tensor("pSum2", [128, QT], F32))
                    pA = eB(nc.psum_tensor("pA", [128, QT], F32))
                    pB = eB(nc.psum_tensor("pB", [128, QT], F32))

                    sch = Sch(cnt)
                    bw = list(bwaits(dict(cnt)))
                    cp_done, sq_done, st_pe = {}, {}, {}
                    for fc in range(FC):
                        sl = fc % 2
                        lx = f"lx{sl}"
                        buf = xc[:, sl * QT:sl * QT + QT]
                        w = list(bw) if fc == 0 else []
                        if fc >= 2:
                            w += [("dve", cp_done[fc - 2]), ("act", sq_done[fc - 2])]
                        sch.emit("sync", partial(lambda h, b, f: dma(h, b, d_x[f * 128:(f + 1) * 128, :]),
                                                 b=buf, f=fc), waits=w, inc=lx, amt=16)
                        ldv = sch.at(lx)
                        w = [(lx, ldv)] + (bw if fc == 0 else [])
                        sch.emit("vector", partial(
                            lambda h, b, f: h.tensor_copy(xln[:, f * QT:(f + 1) * QT], b),
                            b=buf, f=fc), waits=w, inc="dve")
                        cp_done[fc] = sch.at("dve")
                        w2 = list(w)
                        if fc >= 1:
                            w2.append(("pe", st_pe[fc - 1]))
                        sch.emit("scalar", partial(lambda h, b: h.activation(xsq[:], b, AF.Square), b=buf),
                                 waits=w2, inc="act")
                        sq_done[fc] = sch.at("act")
                        for qh in range(2):
                            sch.emit("tensor", partial(
                                lambda h, q, f: h.matmul(pSum1[:, q * 512:(q + 1) * 512], c_ones[:],
                                                         xln[:, f * QT + q * 512:f * QT + q * 512 + 512],
                                                         start=(f == 0), stop=(f == FC - 1)),
                                q=qh, f=fc),
                                waits=([("dve", cp_done[fc])] + (bw if fc == 0 else [])))
                            sch.emit("tensor", partial(
                                lambda h, q, f: h.matmul(pSum2[:, q * 512:(q + 1) * 512], c_ones[:],
                                                         xsq[:, q * 512:(q + 1) * 512],
                                                         start=(f == 0), stop=(f == FC - 1)),
                                q=qh, f=fc),
                                waits=[("act", sq_done[fc])],
                                inc="pe" if qh == 1 else None)
                        st_pe[fc] = sch.at("pe")
                    pe_stats = sch.at("pe")
                    if _SUB < 2:
                        run_phase(nc, sems, sch)
                        _finish(nc, sems, cnt)
                        return nc, dbg
                    sch.emit("vector", lambda h: h.tensor_scalar(tmp[:], pSum1[:], 1.0 / H, None, OP.mult),
                             waits=[("pe", pe_stats)], inc="dve")
                    sch.emit("vector", lambda h: h.tensor_scalar(xc[:, 0:QT], pSum2[:], 1.0 / H, None,
                                                                 OP.mult), inc="dve")
                    sch.emit("vector", lambda h: h.tensor_tensor(pA[:], tmp[:], tmp[:], OP.mult),
                             inc="dve")
                    sch.emit("vector", lambda h: h.tensor_tensor(pB[:], xc[:, 0:QT], pA[:], OP.subtract),
                             inc="dve")
                    var_done = sch.at("dve")
                    sch.emit("scalar", lambda h: h.activation(tmp[:], pB[:], AF.Ln, bias=c_eps[:, 0:1]),
                             waits=[("dve", var_done)], inc="act")
                    sch.emit("scalar", lambda h: h.activation(pA[:], tmp[:], AF.Exp, scale=-0.5), inc="act")
                    a_done = sch.at("act")
                    sch.emit("vector", lambda h: h.tensor_scalar(tmp[:], pSum1[:], 1.0 / H, None, OP.mult),
                             waits=[("act", a_done)], inc="dve")
                    sch.emit("vector", lambda h: h.scalar_tensor_tensor(pB[:], tmp[:], -1.0, pA[:],
                                                                       OP.mult, OP.mult), inc="dve")
                    b_done = sch.at("dve")

                    for fc in range(FC):
                        sch.emit("vector", partial(
                            lambda h, f: h.tensor_tensor(tmp[:], xln[:, f * QT:(f + 1) * QT],
                                                         pA[:], OP.mult), f=fc),
                            waits=[("dve", b_done)], inc="dve")
                        sch.emit("vector", partial(lambda h, f: h.tensor_tensor(
                            xln[:, f * QT:(f + 1) * QT], tmp[:], pB[:], OP.add), f=fc), inc="dve")
                    xln_done = sch.at("dve")
                    if _SUB < 3:
                        run_phase(nc, sems, sch)
                        _finish(nc, sems, cnt)
                        return nc, dbg

                    pO = [pSum1, pSum2]
                    col_order = (list(range(32, 96)) + list(range(32)))
                    if _SUB == 3:
                        col_order = col_order[:4]
                    wmm_done, ev_dve, kv_st = {}, {}, {}
                    cvt_done = {}
                    kv_done_w = []

                    def emit_wload(i):
                        c = col_order[i]
                        sl = i % 3
                        lw = f"lw{sl}"
                        w = []
                        if i >= 3:
                            w.append(("pe", wmm_done.get(sl, 0)))
                        sch.emit("gpsimd", partial(
                            lambda h, s, c2: dma(
                                h, wbf[:, s * 4096:(s + 1) * 4096],
                                d_qkvw[c2].rearrange("p a b -> p (a b)")),
                            s=sl, c2=c), waits=w, inc=lw, amt=16)
                        cvt_done[i] = (lw, sch.at(lw))

                    emit_wload(0)
                    emit_wload(1)
                    for idx, c in enumerate(col_order):
                        sl = idx % 2
                        wsl = idx % 3
                        po = pO[sl]
                        if idx + 2 < len(col_order):
                            emit_wload(idx + 2)
                        w = [cvt_done[idx]]
                        if idx < 2:
                            w.append(("dve", xln_done))
                        if idx >= 2:
                            w.append(("dve", ev_dve.get(idx - 2, 0)))
                        for fc in range(FC):
                            for qh in range(2):
                                last = (fc == FC - 1 and qh == 1)
                                sch.emit("tensor", partial(
                                    lambda h, p_, s, f, q, first, lst: h.matmul(
                                        p_[:, q * 512:(q + 1) * 512],
                                        wbf[:, s * 4096 + f * 128:s * 4096 + f * 128 + 128],
                                        xln[:, f * QT + q * 512:f * QT + q * 512 + 512],
                                        start=first, stop=lst),
                                    p_=po, s=wsl, f=fc, q=qh, first=(fc == 0), lst=(fc == FC - 1)),
                                    waits=(w if fc == 0 and qh == 0 else ()),
                                    inc="pe" if last else None)
                        wmm_done[wsl] = sch.at("pe")
                        pev = sch.at("pe")
                        if c < 32:
                            sch.emit("vector", partial(
                                lambda h, c2, p_: h.tensor_scalar(
                                    qctx[:, c2 * QT:(c2 + 1) * QT], p_[:], c_qkvb[:, c2:c2 + 1], None, OP.add),
                                c2=c, p_=po), waits=[("pe", pev)], inc="dve")
                            ev_dve[idx] = sch.at("dve")
                        else:
                            lk = f"lk{sl}"
                            w3 = [("pe", pev)]
                            if idx >= 2:
                                w3.append((lk, kv_st.get(sl, 0)))
                            sch.emit("vector", partial(
                                lambda h, c2, p_, s: h.tensor_scalar(
                                    kevac[:, s * QT:(s + 1) * QT], p_[:], c_qkvb[:, c2:c2 + 1], None, OP.add),
                                c2=c, p_=po, s=sl), waits=w3, inc="dve")
                            ev_dve[idx] = sch.at("dve")
                            dloc = d_k_loc if c < 64 else d_v_loc
                            roff = (c - 32) * 128 if c < 64 else (c - 64) * 128
                            sch.emit("sync", partial(
                                lambda h, r2, s, dl: dma(h, dl.ap()[r2:r2 + 128, :],
                                                         kevac[:, s * QT:(s + 1) * QT]),
                                r2=roff, s=sl, dl=dloc), waits=[("dve", ev_dve[idx])], inc=lk, amt=16)
                            kv_st[sl] = sch.at(lk)
                        if idx == 31 and stage >= 2:
                            sch.emit("gpsimd", lambda h: h.collective_compute(
                                "AllGather", OP.bypass, replica_groups=[[0, 1, 2, 3], [4, 5, 6, 7]],
                                ins=[d_k_loc.ap().opt()], outs=[d_k_full.ap().opt()]),
                                waits=[("lk0", sch.at("lk0")), ("lk1", sch.at("lk1"))] + bw, inc="cc")
                        if idx == 63:
                            kv_done_w = [("lk0", sch.at("lk0")), ("lk1", sch.at("lk1"))]
                            if stage >= 2:
                                sch.emit("gpsimd", lambda h: h.collective_compute(
                                    "AllGather", OP.bypass, replica_groups=[[0, 1, 2, 3], [4, 5, 6, 7]],
                                    ins=[d_v_loc.ap().opt()], outs=[d_v_full.ap().opt()]),
                                    waits=kv_done_w, inc="cc")
                    q_done = sch.at("dve")
                    if _SUB < 5:
                        run_phase(nc, sems, sch)
                        _finish(nc, sems, cnt)
                        return nc, dbg

                    if stage < STAGE_FULL:
                        for fc in range(FC):
                            sch.emit("sync", partial(lambda h, f: dma(
                                h, dbg["xln"][f * 128:(f + 1) * 128, :], xln[:, f * QT:(f + 1) * QT]),
                                f=fc), waits=[("dve", xln_done)], inc="ld", amt=16)
                        for c in range(32):
                            sch.emit("sync", partial(lambda h, c2: dma(
                                h, dbg["q"][c2 * 128:(c2 + 1) * 128, :], qctx[:, c2 * QT:(c2 + 1) * QT]),
                                c2=c), waits=[("dve", q_done)], inc="ld", amt=16)
                        sch.emit("sync", lambda h: dma(h, dbg["kv"][0:H, :], d_k_loc.ap()),
                                 waits=list(kv_done_w), inc="ld", amt=16)
                        sch.emit("sync", lambda h: dma(h, dbg["kv"][H:2 * H, :], d_v_loc.ap()),
                                 waits=list(kv_done_w), inc="ld", amt=16)
                    run_phase(nc, sems, sch)

                if stage < 2:
                    _finish(nc, sems, cnt)
                    return nc, dbg

                with ExitStack() as ctxC:
                    eC = ctxC.enter_context
                    kbuf = eC(nc.sbuf_tensor("kbuf", [128, 2 * SEQ], BF16))
                    vbbuf = eC(nc.sbuf_tensor("vbbuf", [128, 2 * SEQ], BF16))
                    vfbuf = eC(nc.sbuf_tensor("vfbuf", [128, SEQ], F32))
                    vtm = eC(nc.sbuf_tensor("vtm", [128, 2 * SEQ], BF16))
                    ptb = eC(nc.sbuf_tensor("ptb", [128, 2 * QT], BF16))
                    rb = eC(nc.sbuf_tensor("rb", [128, QT], F32))
                    sqb = eC(nc.sbuf_tensor("sqb", [128, SEQ], BF16))
                    att_tmp = eC(nc.sbuf_tensor("att_tmp", [128, QT], F32))

                    pS = [eC(nc.psum_tensor("pS0", [128, QT], F32)),
                          eC(nc.psum_tensor("pS1", [128, QT], F32))]
                    pC = eC(nc.psum_tensor("pCc", [128, QT], F32))
                    pD = eC(nc.psum_tensor("pDd", [128, QT], F32))

                    sch = Sch(cnt)
                    bw = list(bwaits(dict(cnt)))

                    qn_pe = 0
                    for hh in range(HEADS):
                        sch.emit("vector", partial(
                            lambda h, h2: h.tensor_tensor(sqb[:, 0:QT], qctx[:, h2 * QT:(h2 + 1) * QT],
                                                          qctx[:, h2 * QT:(h2 + 1) * QT], OP.mult), h2=hh),
                            waits=(bw if hh == 0 else [("pe", qn_pe)]), inc="dve")
                        sq_d = sch.at("dve")
                        for qh in range(2):
                            sch.emit("tensor", partial(
                                lambda h, q: h.matmul(pS[0][:, q * 512:(q + 1) * 512], c_ones[:],
                                                      sqb[:, q * 512:(q + 1) * 512], start=True, stop=True),
                                q=qh), waits=([("dve", sq_d)] + (bw if hh == 0 else [])),
                                inc="pe" if qh == 1 else None)
                        qn_pe = sch.at("pe")
                        sch.emit("vector", partial(
                            lambda h, h2: h.tensor_reduce(t_mhq[:, h2:h2 + 1], pS[0][:], AXX, OP.max),
                            h2=hh), waits=[("pe", qn_pe)], inc="dve")
                    qnorm_dve = sch.at("dve")
                    if _ASUB < 2:
                        run_phase(nc, sems, sch)
                        _finish(nc, sems, cnt)
                        return nc, dbg

                    ccv = sch.at("cc")
                    kv_ld, prep_done, negm_done = {}, {}, {}
                    head_pe_all, ctx_evac, sums_read = {}, {}, {}

                    def emit_kv_load(hh):
                        slot = hh % 2
                        lv = f"lv{slot}"
                        w = [("cc", 1)]
                        if hh < 2:
                            w += bw
                        else:
                            w.append(("pe", head_pe_all[hh - 2]))
                            w.append(("dve", prep_done[hh - 2]))
                            w.append(("pool", prep_pool.get(hh - 2, 0)))
                        for r in range(NRANK):
                            sch.emit("sync", partial(
                                lambda h, h2, r2, s2: dma(
                                    h, kbuf[:, s2 * SEQ + r2 * QT: s2 * SEQ + (r2 + 1) * QT],
                                    d_k_full.ap()[r2 * H + h2 * 128: r2 * H + h2 * 128 + 128, :]),
                                h2=hh, r2=r, s2=slot), waits=(w if r == 0 else ()), inc=lv, amt=16)
                        sch.emit("sync", fn=None, waits=[("cc", ccv)])
                        for r in range(NRANK):
                            sch.emit("sync", partial(
                                lambda h, h2, r2, s2: dma(
                                    h, vbbuf[:, s2 * SEQ + r2 * QT: s2 * SEQ + (r2 + 1) * QT],
                                    d_v_full.ap()[r2 * H + h2 * 128: r2 * H + h2 * 128 + 128, :]),
                                h2=hh, r2=r, s2=slot), inc=lv, amt=16)
                        kv_ld[hh] = (lv, sch.at(lv))

                    prep_pool = {}
                    tr_last_pe = {}
                    nrm_last_pe = {-1: qn_pe}

                    def emit_prep_pre(hh):
                        slot = hh % 2
                        for half in range(2):
                            sch.emit("gpsimd", partial(
                                lambda h, s2, hf: h.tensor_copy(
                                    vfbuf[:, hf * 2048:(hf + 1) * 2048],
                                    vbbuf[:, s2 * SEQ + hf * 2048:s2 * SEQ + (hf + 1) * 2048]),
                                s2=slot, hf=half),
                                waits=([kv_ld[hh], ("pe", tr_last_pe.get(hh - 1, 0))]
                                       if half == 0 else ()), inc="pool")
                        sch.emit("gpsimd", partial(
                            lambda h, s2: h.tensor_tensor(
                                sqb[:], kbuf[:, s2 * SEQ:(s2 + 1) * SEQ],
                                kbuf[:, s2 * SEQ:(s2 + 1) * SEQ], OP.mult),
                            s2=slot),
                            waits=[("pe", nrm_last_pe.get(hh - 1, 0))], inc="pool")
                        prep_pool[hh] = sch.at("pool")

                    def emit_prep(hh, act_gate=None):
                        slot = hh % 2
                        vf_d = ("pool", prep_pool[hh])
                        evac_d = {}
                        for grp in range(4):
                            ps = pS[grp % 2]
                            w = [vf_d]
                            if grp >= 2:
                                w.append(("dve", evac_d[grp - 2]))
                            if act_gate is not None:
                                w.append(act_gate)
                            else:
                                w.append(("dve", qnorm_dve))
                            for j in range(8):
                                kc = grp * 8 + j
                                sch.emit("tensor", partial(
                                    lambda h, ps_, j2, kc2: h.transpose(
                                        ps_[:, j2 * 128:(j2 + 1) * 128],
                                        vfbuf[:, kc2 * 128:(kc2 + 1) * 128], c_ident[:]),
                                    ps_=ps, j2=j, kc2=kc),
                                    waits=(w if j == 0 else ()), inc="pe" if j == 7 else None)
                            pe_v = sch.at("pe")
                            sch.emit("vector", partial(
                                lambda h, ps_, s2, g2: h.tensor_copy(
                                    vtm[:, s2 * SEQ + g2 * QT: s2 * SEQ + (g2 + 1) * QT], ps_[:]),
                                ps_=ps, s2=slot, g2=grp), waits=[("pe", pe_v)], inc="dve")
                            evac_d[grp] = sch.at("dve")
                        tr_last_pe[hh] = sch.at("pe")
                        for rnd in range(2):
                            w0 = ([("pool", prep_pool[hh]), ("dve", evac_d[3])]
                                  if rnd == 0 else [("dve", sch.at("dve"))])
                            for j in range(4):
                                sch.emit("tensor", partial(
                                    lambda h, ps_, j2, r2: h.matmul(
                                        ps_[:, (j2 % 2) * 512:(j2 % 2) * 512 + 512], c_ones[:],
                                        sqb[:, r2 * 2048 + j2 * 512: r2 * 2048 + (j2 + 1) * 512],
                                        start=True, stop=True),
                                    ps_=pS[j // 2], j2=j, r2=rnd),
                                    waits=(w0 if j == 0 else ()), inc="pe" if j == 3 else None)
                            pe_v = sch.at("pe")
                            nrm_last_pe[hh] = pe_v
                            for half in range(2):
                                first = (rnd == 0 and half == 0)
                                sch.emit("vector", partial(
                                    lambda h, ps_, c2: h.tensor_reduce(t_sc1[:, c2:c2 + 1], ps_[:],
                                                                       AXX, OP.max),
                                    ps_=pS[half], c2=(0 if first else 1)),
                                    waits=[("pe", pe_v)], inc="dve")
                                if not first:
                                    sch.emit("vector", lambda h: h.tensor_tensor(
                                        t_sc1[:, 0:1], t_sc1[:, 0:1], t_sc1[:, 1:2], OP.max), inc="dve")
                        sch.emit("vector", partial(
                            lambda h, h2: h.tensor_tensor(t_sc1[:, 1:2], t_sc1[:, 0:1],
                                                          t_mhq[:, h2:h2 + 1], OP.mult), h2=hh),
                            waits=[("dve", qnorm_dve)], inc="dve")
                        mm_d = sch.at("dve")
                        sch.emit("scalar", lambda h: h.activation(t_sc1[:, 0:1], t_sc1[:, 1:2], AF.Ln),
                                 waits=[("dve", mm_d)], inc="act")
                        sch.emit("scalar", lambda h: h.activation(t_sc1[:, 1:2], t_sc1[:, 0:1], AF.Exp,
                                                                  scale=0.5), inc="act")
                        a_v = sch.at("act")
                        sch.emit("vector", partial(
                            lambda h, s2: h.tensor_scalar(t_negm[:, s2:s2 + 1], t_sc1[:, 1:2],
                                                          -SCALE * 1.05, None, OP.mult), s2=slot),
                            waits=[("act", a_v)], inc="dve")
                        negm_done[hh] = sch.at("dve")
                        prep_done[hh] = sch.at("dve")

                    emit_kv_load(0)
                    emit_kv_load(1)
                    emit_prep_pre(0)
                    emit_prep(0)
                    if _ASUB < 3:
                        run_phase(nc, sems, sch)
                        _finish(nc, sems, cnt)
                        return nc, dbg
                    _NH = HEADS if _ASUB >= 5 else {3: 1, 35: 2, 4: 4}.get(_ASUB, 4)

                    for hh in range(_NH):
                        slot = hh % 2
                        exp_done, ctx_pe = {}, {}

                        def emit_ctx_sums(kc, _hh=None, _slot=None):
                            _hh = hh if _hh is None else _hh
                            _slot = slot if _slot is None else _slot
                            w = [("act", exp_done[kc])]
                            if kc == 0:
                                w.append(("dve", ctx_evac.get(_hh - 1, 0)))
                                w.append(("act", sums_read.get(_hh - 1, 0)))
                            for qh in range(2):
                                sch.emit("tensor", partial(
                                    lambda h, s2, kc2, q: h.matmul(
                                        pC[:, q * 512:(q + 1) * 512],
                                        vtm[:, s2 * SEQ + kc2 * 128:s2 * SEQ + (kc2 + 1) * 128],
                                        ptb[:, (kc2 % 2) * QT + q * 512:(kc2 % 2) * QT + q * 512 + 512],
                                        start=(kc2 == 0), stop=(kc2 == KC - 1)),
                                    s2=_slot, kc2=kc, q=qh), waits=(w if qh == 0 else ()))
                                sch.emit("tensor", partial(
                                    lambda h, kc2, q: h.matmul(
                                        pD[:, q * 512:(q + 1) * 512], c_ones[:],
                                        ptb[:, (kc2 % 2) * QT + q * 512:(kc2 % 2) * QT + q * 512 + 512],
                                        start=(kc2 == 0), stop=(kc2 == KC - 1)),
                                    kc2=kc, q=qh), inc="pe" if qh == 1 else None)
                            ctx_pe[kc] = sch.at("pe")

                        for kc in range(KC):
                            ps = pS[kc % 2]
                            w = []
                            if kc < 2:
                                w.append(("dve", prep_done[hh]))
                            else:
                                w.append(("act", exp_done[kc - 2]))
                            for qh in range(2):
                                sch.emit("tensor", partial(
                                    lambda h, ps_, s2, kc2, q, h2: h.matmul(
                                        ps_[:, q * 512:(q + 1) * 512],
                                        kbuf[:, s2 * SEQ + kc2 * 128: s2 * SEQ + (kc2 + 1) * 128],
                                        qctx[:, h2 * QT + q * 512: h2 * QT + q * 512 + 512],
                                        start=True, stop=True),
                                    ps_=ps, s2=slot, kc2=kc, q=qh, h2=hh),
                                    waits=(w if qh == 0 else ()), inc="pe" if qh == 1 else None)
                            s_pe = sch.at("pe")
                            w = [("pe", s_pe), ("dve", negm_done[hh])]
                            if kc >= 2:
                                w.append(("pe", ctx_pe[kc - 2]))
                            sch.emit("scalar", partial(
                                lambda h, ps_, kc2, s2: h.activation(
                                    ptb[:, (kc2 % 2) * QT:(kc2 % 2) * QT + QT], ps_[:], AF.Exp,
                                    bias=t_negm[:, s2:s2 + 1], scale=SCALE),
                                ps_=ps, kc2=kc, s2=slot), waits=w, inc="act")
                            exp_done[kc] = sch.at("act")
                            if kc >= 1:
                                emit_ctx_sums(kc - 1)
                            if kc == 4 and hh + 1 < _NH:
                                emit_prep_pre(hh + 1)
                        emit_ctx_sums(KC - 1)
                        head_pe_all[hh] = sch.at("pe")
                        if hh + 2 < _NH:
                            emit_kv_load(hh + 2)
                        if hh + 1 < _NH:
                            emit_prep(hh + 1, act_gate=("act", exp_done[KC - 1]))
                        sch.emit("scalar", lambda h: h.activation(att_tmp[:], pD[:], AF.Ln, bias=c_eps[:, 1:2]),
                                 waits=[("pe", head_pe_all[hh])], inc="act")
                        sums_read[hh] = sch.at("act")
                        sch.emit("scalar", lambda h: h.activation(rb[:], att_tmp[:], AF.Exp, scale=-1.0),
                                 waits=[("dve", ctx_evac.get(hh - 1, 0))], inc="act")
                        rb_done = sch.at("act")
                        sch.emit("vector", partial(
                            lambda h, h2: h.tensor_tensor(qctx[:, h2 * QT:(h2 + 1) * QT], pC[:], rb[:],
                                                          OP.mult), h2=hh),
                            waits=[("act", rb_done)], inc="dve")
                        ctx_evac[hh] = sch.at("dve")
                    ctx_done = sch.at("dve")

                    if stage < STAGE_FULL:
                        for c in range(32):
                            sch.emit("sync", partial(lambda h, c2: dma(
                                h, dbg["ctx"][c2 * 128:(c2 + 1) * 128, :], qctx[:, c2 * QT:(c2 + 1) * QT]),
                                c2=c), waits=[("dve", ctx_done)], inc="ld", amt=16)
                    run_phase(nc, sems, sch)

                if stage < 3:
                    _finish(nc, sems, cnt)
                    return nc, dbg

                ctxD = ExitStack()
                with ExitStack() as ctxP:
                    eP = ctxP.enter_context
                    eD = ctxD.enter_context
                    wbf = eD(nc.sbuf_tensor("wbf2", [128, 2 * 4096], BF16))
                    xc = eD(nc.sbuf_tensor("xc2", [128, 2 * QT], F32))
                    x2b = eD(nc.sbuf_tensor("x2b", [128, 2 * QT], F32))
                    x2bf = eD(nc.sbuf_tensor("x2bf", [128, QT], BF16))
                    x2sq = eD(nc.sbuf_tensor("x2sq", [128, QT], BF16))
                    tmp = eD(nc.sbuf_tensor("tmpf2", [128, QT], F32))

                    pO = [eP(nc.psum_tensor("pO0", [128, QT], F32)),
                          eP(nc.psum_tensor("pO1", [128, QT], F32))]
                    pSum1 = eP(nc.psum_tensor("pSum1b", [128, QT], F32))
                    pSum2 = eP(nc.psum_tensor("pSum2b", [128, QT], F32))

                    sch = Sch(cnt)
                    bw = list(bwaits(dict(cnt)))
                    p5_bw = list(bw)
                    wmm_done, ev_dve, x2_st = {}, {}, {}
                    cp_d, sq_d, stats_pe = {}, {}, {}

                    def emit_stats(c):
                        for qh in range(2):
                            sch.emit("tensor", partial(
                                lambda h, q, c2: h.matmul(
                                    pSum1[:, q * 512:(q + 1) * 512], c_ones[:],
                                    x2bf[:, q * 512:(q + 1) * 512],
                                    start=(c2 == 0), stop=(c2 == FC - 1)),
                                q=qh, c2=c), waits=([("dve", cp_d[c])] if qh == 0 else ()))
                            sch.emit("tensor", partial(
                                lambda h, q, c2: h.matmul(
                                    pSum2[:, q * 512:(q + 1) * 512], c_ones[:],
                                    x2sq[:, q * 512:(q + 1) * 512], start=(c2 == 0), stop=(c2 == FC - 1)),
                                q=qh, c2=c), waits=([("act", sq_d[c])] if qh == 0 else ()),
                                inc="pe" if qh == 1 else None)
                        stats_pe[c] = sch.at("pe")

                    cvt_done = {}

                    def emit_wload(i):
                        sl = i % 2
                        lw = f"lw{sl}"
                        w = list(bw) if i < 2 else [("pe", wmm_done.get(sl, 0))]
                        sch.emit("gpsimd", partial(
                            lambda h, s, c2: dma(
                                h, wbf[:, s * 4096:(s + 1) * 4096],
                                d_ow[c2].rearrange("p a b -> p (a b)")),
                            s=sl, c2=i), waits=w, inc=lw, amt=16)
                        cvt_done[i] = (lw, sch.at(lw))

                    emit_wload(0)
                    for c in range(FC):
                        sl = c % 2
                        wsl = c % 2
                        po = pO[sl]
                        lx = f"lx{sl}"
                        w = list(bw) if c == 0 else ([("dve", ev_dve[c - 2])] if c >= 2 else [])
                        sch.emit("sync", partial(
                            lambda h, c2, s: dma(h, xc[:, s * QT:(s + 1) * QT],
                                                 d_x[c2 * 128:(c2 + 1) * 128, :]),
                            c2=c, s=sl), waits=w, inc=lx, amt=16)
                        x1_ld = sch.at(lx)
                        if c + 1 < FC:
                            emit_wload(c + 1)
                        w = [cvt_done[c]]
                        if c >= 2:
                            w.append(("dve", ev_dve[c - 2]))
                        for fc in range(FC):
                            for qh in range(2):
                                last = (fc == FC - 1 and qh == 1)
                                sch.emit("tensor", partial(
                                    lambda h, p_, s, f, q, first, lst: h.matmul(
                                        p_[:, q * 512:(q + 1) * 512],
                                        wbf[:, s * 4096 + f * 128:s * 4096 + f * 128 + 128],
                                        qctx[:, f * QT + q * 512:f * QT + q * 512 + 512],
                                        start=first, stop=lst),
                                    p_=po, s=wsl, f=fc, q=qh, first=(fc == 0), lst=(fc == FC - 1)),
                                    waits=(w if fc == 0 and qh == 0 else ()),
                                    inc="pe" if last else None)
                        wmm_done[wsl] = sch.at("pe")
                        pev = sch.at("pe")
                        if c >= 1:
                            emit_stats(c - 1)
                        ls = f"ls{sl}"
                        w = [("pe", pev), (lx, x1_ld)]
                        if c >= 2:
                            w.append((ls, x2_st.get(sl, 0)))
                        sch.emit("vector", partial(
                            lambda h, c2, s, p_: h.scalar_tensor_tensor(
                                x2b[:, s * QT:(s + 1) * QT], p_[:], c_ob[:, c2:c2 + 1],
                                xc[:, s * QT:(s + 1) * QT], OP.add, OP.add),
                            c2=c, s=sl, p_=po), waits=w, inc="dve")
                        x2_d = sch.at("dve")
                        sch.emit("sync", partial(
                            lambda h, c2, s: dma(h, d_x2.ap()[c2 * 128:(c2 + 1) * 128, :],
                                                 x2b[:, s * QT:(s + 1) * QT]),
                            c2=c, s=sl), waits=[("dve", x2_d)], inc=ls, amt=16)
                        x2_st[sl] = sch.at(ls)
                        w = []
                        if c >= 1:
                            w.append(("pe", stats_pe[c - 1]))
                        sch.emit("vector", partial(
                            lambda h, s: h.tensor_copy(x2bf[:], x2b[:, s * QT:(s + 1) * QT]), s=sl),
                            waits=w, inc="dve")
                        cp_d[c] = sch.at("dve")
                        sch.emit("scalar", partial(
                            lambda h, s: h.activation(x2sq[:], x2b[:, s * QT:(s + 1) * QT], AF.Square),
                            s=sl), waits=[("dve", x2_d)] +
                            ([("pe", stats_pe[c - 1])] if c >= 1 else []), inc="act")
                        sq_d[c] = sch.at("act")
                        ev_dve[c] = sch.at("dve")
                    emit_stats(FC - 1)
                    x2_all_w = [("ls0", sch.at("ls0")), ("ls1", sch.at("ls1"))]
                    pe_stats = sch.at("pe")
                    sch.emit("vector", lambda h: h.tensor_scalar(tmp[:], pSum2[:], 1.0 / H, None, OP.mult),
                             waits=[("pe", pe_stats), ("dve", ev_dve[FC - 1])], inc="dve")
                    sch.emit("vector", lambda h: h.tensor_scalar(x2b[:, 0:QT], pSum1[:], 1.0 / H, None,
                                                                 OP.mult), inc="dve")
                    sch.emit("vector", lambda h: h.tensor_tensor(pO[0][:], x2b[:, 0:QT], x2b[:, 0:QT],
                                                                 OP.mult), inc="dve")
                    sch.emit("vector", lambda h: h.tensor_tensor(tmp[:], tmp[:], pO[0][:], OP.subtract),
                             inc="dve")
                    var_d = sch.at("dve")
                    sch.emit("scalar", lambda h: h.activation(x2b[:, QT:2 * QT], tmp[:], AF.Ln,
                                                              bias=c_eps[:, 0:1]),
                             waits=[("dve", var_d)], inc="act")
                    sch.emit("scalar", lambda h: h.activation(pO[0][:], x2b[:, QT:2 * QT], AF.Exp,
                                                              scale=-0.5), inc="act")
                    a2_act = sch.at("act")
                    sch.emit("vector", lambda h: h.scalar_tensor_tensor(pO[1][:], x2b[:, 0:QT], -1.0,
                                                                        pO[0][:], OP.mult, OP.mult),
                             waits=[("act", a2_act)], inc="dve")
                    b2_d = sch.at("dve")

                    if stage < STAGE_FULL:
                        sch.emit("sync", lambda h: dma(h, dbg["x2"], d_x2.ap()),
                                 waits=list(x2_all_w), inc="ld", amt=16)
                    run_phase(nc, sems, sch)
                    ctxD.close()

                    if stage < 4:
                        _finish(nc, sems, cnt)
                        return nc, dbg

                    with ExitStack() as ctxE:
                        eE = ctxE.enter_context
                        xln2 = qctx
                        dstage = eE(nc.sbuf_tensor("dstage", [128, 2 * 29 * 128], BF16))
                        hbuf = eE(nc.sbuf_tensor("hbuf", [128, 29 * QT], BF16))
                        sg = eE(nc.sbuf_tensor("sg", [128, QT], BF16))
                        fxc = eE(nc.sbuf_tensor("fxc", [128, 2 * QT], F32))
                        fyb = eE(nc.sbuf_tensor("fyb", [128, 2 * QT], F32))
                        fn1 = eE(nc.sbuf_tensor("fn1", [128, QT], F32))

                        sch = Sch(cnt)
                        dump_w = [("ld", sch.at("ld"))] if stage < STAGE_FULL else []
                        for fc in range(FC):
                            sch.emit("gpsimd", partial(
                                lambda h, f: dma(h, xln2[:, f * QT:(f + 1) * QT],
                                                 d_x2.ap()[f * 128:(f + 1) * 128, :]),
                                f=fc),
                                waits=([("pe", pe_stats)] + list(x2_all_w) + dump_w
                                       if fc == 0 else ()), inc="lk0", amt=16)
                        x2cast_done = sch.at("lk0")
                        norm_done = {}
                        for fc in range(FC):
                            w = [("dve", b2_d)]
                            if fc == 0:
                                w.append(("lk0", x2cast_done))
                            sch.emit("vector", partial(
                                lambda h, f: h.tensor_tensor(fn1[:], xln2[:, f * QT:(f + 1) * QT],
                                                             pO[0][:], OP.mult), f=fc),
                                waits=w, inc="dve")
                            sch.emit("vector", partial(
                                lambda h, f: h.tensor_tensor(xln2[:, f * QT:(f + 1) * QT], fn1[:],
                                                             pO[1][:], OP.add), f=fc), inc="dve")
                            norm_done[fc] = sch.at("dve")
                        xln2_done = sch.at("dve")

                        if stage < STAGE_FULL:
                            for fc in range(FC):
                                sch.emit("sync", partial(lambda h, f: dma(
                                    h, dbg["xln2"][f * 128:(f + 1) * 128, :], xln2[:, f * QT:(f + 1) * QT]),
                                    f=fc), waits=[("dve", xln2_done)], inc="ld", amt=16)

                        pG = [pSum1, pO[0]]
                        pU = [pSum2, pO[1]]
                        pDn = [pSum1, pSum2]
                        gu_slot_pe = {}
                        pg_slot_dve = {}
                        dn_pe_done, ev_dve2, y_st2 = {}, {}, {}
                        down_ev_tail = 0
                        last_h_done = 0
                        prev_pass_st_w = []
                        for pi, (rc0, rc1) in enumerate(FFN_PASSES):
                            nrc = rc1 - rc0
                            for j in range(nrc):
                                rc = rc0 + j
                                sl = j % 2
                                gg = f"gg{sl}"
                                w = [("pe", gu_slot_pe.get(sl, 0))]
                                for (stg, src) in ((gstage, d_gw), (ustage, d_uw)):
                                    sch.emit("gpsimd", partial(
                                        lambda h, st_, sr_, r2, s2: dma(
                                            h, st_[:, s2 * 4096:(s2 + 1) * 4096],
                                            sr_[r2].rearrange("p a b -> p (a b)")),
                                        st_=stg, sr_=src, r2=rc, s2=sl), waits=w, inc=gg, amt=16)
                                gu_ldv = sch.at(gg)
                                wmm = [(gg, gu_ldv), ("dve", pg_slot_dve.get(sl, 0))]
                                if pi == 0 and j < 2:
                                    wmm.append(("dve", xln2_done))
                                if j < 2 and pi > 0:
                                    wmm.append(("dve", down_ev_tail))
                                for (pp, stg) in ((pG, gstage), (pU, ustage)):
                                    for fc in range(FC):
                                        for qh in range(2):
                                            last = (fc == FC - 1 and qh == 1)
                                            sch.emit("tensor", partial(
                                                lambda h, pp_, st_, s, f, q, first, lst: h.matmul(
                                                    pp_[s][:, q * 512:(q + 1) * 512],
                                                    st_[:, s * 4096 + f * 128:s * 4096 + f * 128 + 128],
                                                    xln2[:, f * QT + q * 512:f * QT + q * 512 + 512],
                                                    start=first, stop=lst),
                                                pp_=pp, st_=stg, s=sl, f=fc, q=qh,
                                                first=(fc == 0), lst=(fc == FC - 1)),
                                                waits=(wmm if (fc == 0 and qh == 0 and stg is gstage) else ()),
                                                inc=("pe" if (last and stg is ustage) else None))
                                gu_slot_pe[sl] = sch.at("pe")
                                gu_pe = sch.at("pe")
                                sch.emit("scalar", partial(
                                    lambda h, s, r2: h.activation(sg[:], pG[s][:], AF.Silu,
                                                                  scale=c_gs[:, r2:r2 + 1]),
                                    s=sl, r2=rc),
                                    waits=[("pe", gu_pe), ("dve", last_h_done)], inc="act")
                                sg_dn = sch.at("act")
                                sch.emit("vector", partial(
                                    lambda h, s, r2, j2: h.scalar_tensor_tensor(
                                        hbuf[:, j2 * QT:(j2 + 1) * QT], pU[s][:], c_us[:, r2:r2 + 1],
                                        sg[:], OP.mult, OP.mult),
                                    s=sl, r2=rc, j2=j), waits=[("act", sg_dn)], inc="dve")
                                pg_slot_dve[sl] = sch.at("dve")
                                last_h_done = sch.at("dve")
                            h_all = sch.at("dve")
                            last_pass = (pi == len(FFN_PASSES) - 1)
                            src_res = d_x2 if pi == 0 else d_x3p
                            for c in range(FC):
                                dcc = pi * FC + c
                                sl = c % 2
                                gd = f"gd{sl}"
                                lx = f"lx{sl}"
                                ly = f"ly{sl}"
                                w = [("dve", h_all)] if c < 2 else [("pe", dn_pe_done.get(dcc - 2, 0))]
                                sch.emit("gpsimd", partial(
                                    lambda h, c2, s2, a2, b2_: dma(
                                        h, dstage[:, s2 * 29 * 128:s2 * 29 * 128 + (b2_ - a2) * 128],
                                        d_dw[c2, :, a2:b2_, :].rearrange("p a b -> p (a b)")),
                                    c2=c, s2=sl, a2=rc0, b2_=rc1), waits=w, inc=gd, amt=16)
                                d_ldv = sch.at(gd)
                                w = [("dve", ev_dve2.get(dcc - 2, xln2_done))] + \
                                    (prev_pass_st_w if c < 2 else [])
                                sch.emit("sync", partial(
                                    lambda h, c2, s2, sr_: dma(h, fxc[:, s2 * QT:(s2 + 1) * QT],
                                                               sr_.ap()[c2 * 128:(c2 + 1) * 128, :]),
                                    c2=c, s2=sl, sr_=src_res), waits=w, inc=lx, amt=16)
                                res_ld = sch.at(lx)
                                wmm = [(gd, d_ldv), ("dve", h_all)]
                                if c >= 2:
                                    wmm.append(("dve", ev_dve2.get(dcc - 2, 0)))
                                for j in range(nrc):
                                    for qh in range(2):
                                        last = (j == nrc - 1 and qh == 1)
                                        sch.emit("tensor", partial(
                                            lambda h, s, j2, q, first, lst: h.matmul(
                                                pDn[s][:, q * 512:(q + 1) * 512],
                                                dstage[:, s * 29 * 128 + j2 * 128:
                                                          s * 29 * 128 + j2 * 128 + 128],
                                                hbuf[:, j2 * QT + q * 512:j2 * QT + q * 512 + 512],
                                                start=first, stop=lst),
                                            s=sl, j2=j, q=qh, first=(j == 0), lst=(j == nrc - 1)),
                                            waits=(wmm if (j == 0 and qh == 0) else ()),
                                            inc="pe" if last else None)
                                dn_pe_done[dcc] = sch.at("pe")
                                w = [("pe", dn_pe_done[dcc]), (lx, res_ld)]
                                if y_st2.get(dcc - 2):
                                    w.append(y_st2[dcc - 2])
                                sch.emit("vector", partial(
                                    lambda h, c2, s: h.scalar_tensor_tensor(
                                        fyb[:, s * QT:(s + 1) * QT], pDn[s][:], c_ds[:, c2:c2 + 1],
                                        fxc[:, s * QT:(s + 1) * QT], OP.mult, OP.add),
                                    c2=c, s=sl), waits=w, inc="dve")
                                ev_dve2[dcc] = sch.at("dve")
                                dst = d_y if last_pass else d_x3p.ap()
                                sch.emit("sync", partial(
                                    lambda h, c2, s, dd: dma(h, dd[c2 * 128:(c2 + 1) * 128, :],
                                                             fyb[:, s * QT:(s + 1) * QT]),
                                    c2=c, s=sl, dd=dst), waits=[("dve", ev_dve2[dcc])], inc=ly, amt=16)
                                y_st2[dcc] = (ly, sch.at(ly))
                            down_ev_tail = sch.at("dve")
                            prev_pass_st_w = [("ly0", sch.at("ly0")), ("ly1", sch.at("ly1"))]
                        run_phase(nc, sems, sch)
        finally:
            close_qctx()

        _finish(nc, sems, cnt)
    return nc, dbg


def _finish(nc, sems, cnt):
    sch = Sch(cnt)
    sch.emit("sync", fn=None, waits=bwaits(dict(cnt)))
    run_phase(nc, sems, sch)



def _prep_shared(qkvw, qkvb, ow, ob, gw, gs, uw, us, dw, ds):
    f32 = np.float32
    return {
        "qkvw": np.ascontiguousarray(
            np.asarray(qkvw, f32).reshape(FC, 128, CC_QKV, 128).transpose(2, 1, 0, 3)),
        "qkvb": np.ascontiguousarray(np.asarray(qkvb, f32).reshape(CC_QKV, 128).T),
        "ow": np.ascontiguousarray(
            np.asarray(ow, f32).reshape(FC, 128, FC, 128).transpose(2, 1, 0, 3)),
        "ob": np.ascontiguousarray(np.asarray(ob, f32).reshape(FC, 128).T),
        "gw": np.ascontiguousarray(
            np.asarray(gw, np.int32).reshape(RC, 128, FC, 128).transpose(0, 3, 2, 1)),
        "uw": np.ascontiguousarray(
            np.asarray(uw, np.int32).reshape(RC, 128, FC, 128).transpose(0, 3, 2, 1)),
        "dw": np.ascontiguousarray(
            np.asarray(dw, np.int32).reshape(FC, 128, RC, 128).transpose(0, 3, 2, 1)),
        "gs": np.ascontiguousarray(np.asarray(gs, f32).reshape(RC, 128).T),
        "us": np.ascontiguousarray(np.asarray(us, f32).reshape(RC, 128).T),
        "ds": np.ascontiguousarray(np.asarray(ds, f32).reshape(FC, 128).T),
        "ones_bf": np.ones((128, 128), dtype=ml_dtypes.bfloat16),
        "ident_f32": np.eye(128, dtype=f32),
        "eps_col": np.concatenate([np.full((128, 1), EPS, dtype=f32),
                                   np.full((128, 1), 1e-38, dtype=f32)], axis=1),
    }


def _in_maps(x, shared):
    x = np.asarray(x, np.float32)
    maps = []
    for c in range(8):
        b, qr = c // 4, c % 4
        m = {"x": np.ascontiguousarray(x[b, qr * QT:(qr + 1) * QT, :].T)}
        m.update(shared)
        maps.append(m)
    return maps


def kernel(L0_attn_x, L0_attn_attn_qkvw, L0_attn_attn_qkvb, L0_attn_attn_ow,
           L0_attn_attn_ob, L0_ffn_gate_w, L0_ffn_gate_s, L0_ffn_up_w,
           L0_ffn_up_s, L0_ffn_down_w, L0_ffn_down_s):
    shared = _prep_shared(L0_attn_attn_qkvw, L0_attn_attn_qkvb, L0_attn_attn_ow,
                          L0_attn_attn_ob, L0_ffn_gate_w, L0_ffn_gate_s,
                          L0_ffn_up_w, L0_ffn_up_s, L0_ffn_down_w, L0_ffn_down_s)
    nc, _ = _build(STAGE_FULL)
    maps = _in_maps(L0_attn_x, shared)
    res = run_bass_kernel_spmd(nc, maps, core_ids=list(range(8)))
    out = np.empty((2, SEQ, H), dtype=np.float32)
    for c in range(8):
        b, qr = c // 4, c % 4
        out[b, qr * QT:(qr + 1) * QT, :] = np.asarray(res.results[c]["y"]).T
    return out


# version 12
# speedup vs baseline: 1.0423x; 1.0423x over previous
import os
import numpy as np
import ml_dtypes
import concourse.bass as bass
import concourse.mybir as mybir
from concourse.alu_op_type import AluOpType
from concourse.bass_utils import run_bass_kernel_spmd
from contextlib import ExitStack
from functools import partial

F32 = mybir.dt.float32
BF16 = mybir.dt.bfloat16
I32 = mybir.dt.int32
AF = mybir.ActivationFunctionType
OP = AluOpType
AXX = mybir.AxisListType.X

H = 4096
QT = 1024
FC = 32
HEADS = 32
DFF = 11008
RC = 86
CC_QKV = 96
SEQ = 4096
KC = 32
NRANK = 4
SCALE = 1.0 / np.sqrt(128.0)
EPS = 1e-5

ENGS = ("sync", "gpsimd", "vector", "scalar", "tensor")
FFN_PASSES = [(0, 29), (29, 58), (58, 86)]
STAGE_FULL = 9

SEM_NAMES = ("ld", "lx0", "lx1", "lw0", "lw1", "lw2", "lw3", "lk0", "lk1",
             "lv0", "lv1", "ls0", "ls1", "ly0", "ly1", "gg0", "gg1",
             "gd0", "gd1", "cc", "pe", "act", "dve")


class Sch:
    def __init__(self, cnt):
        self.ops = {e: [] for e in ENGS}
        self.cnt = cnt

    SELF_SEM = {"vector": "dve", "scalar": "act"} if os.environ.get("K_SELFSYNC") else {}

    def emit(self, eng, fn=None, waits=(), inc=None, amt=1):
        waits = list(waits)
        ss = self.SELF_SEM.get(eng)
        if ss is not None and fn is not None and self.cnt.get(ss, 0) > 0:
            waits.append((ss, self.cnt[ss]))
        incs = []
        if inc is not None:
            self.cnt[inc] = self.cnt.get(inc, 0) + amt
            incs = [(inc, amt)]
        self.ops[eng].append((tuple(waits), incs, fn))
        return self.cnt.get(inc, 0)

    def at(self, sem):
        return self.cnt.get(sem, 0)


def run_phase(nc, sems, sch):
    with nc.Block() as block:
        for eng in ENGS:
            ops = sch.ops[eng]
            if not ops:
                continue

            def body(h, _ops=ops):
                waited = {}
                for waits, incs, fn in _ops:
                    for (s, v) in waits:
                        if v > 0 and waited.get(s, -1) < v:
                            h.wait_ge(sems[s], v)
                            waited[s] = v
                    inst = fn(h) if fn is not None else None
                    for (s, a) in incs:
                        inst.then_inc(sems[s], a)
            getattr(block, eng)(body)


def bwaits(cnt):
    return tuple((s, v) for s, v in cnt.items() if v > 0)


def _build(stage=STAGE_FULL):
    nc = bass.Bass("TRN2", target_bir_lowering=False)

    d_x = nc.dram_tensor("x", [H, QT], F32, kind="ExternalInput").ap()
    d_qkvw = nc.dram_tensor("qkvw", [CC_QKV, 128, FC, 128], F32, kind="ExternalInput").ap()
    d_qkvb = nc.dram_tensor("qkvb", [128, CC_QKV], F32, kind="ExternalInput").ap()
    d_ow = nc.dram_tensor("ow", [FC, 128, FC, 128], F32, kind="ExternalInput").ap()
    d_ob = nc.dram_tensor("ob", [128, FC], F32, kind="ExternalInput").ap()
    d_gw = nc.dram_tensor("gw", [RC, 128, FC, 128], I32, kind="ExternalInput").ap()
    d_uw = nc.dram_tensor("uw", [RC, 128, FC, 128], I32, kind="ExternalInput").ap()
    d_dw = nc.dram_tensor("dw", [FC, 128, RC, 128], I32, kind="ExternalInput").ap()
    d_gs = nc.dram_tensor("gs", [128, RC], F32, kind="ExternalInput").ap()
    d_us = nc.dram_tensor("us", [128, RC], F32, kind="ExternalInput").ap()
    d_ds = nc.dram_tensor("ds", [128, FC], F32, kind="ExternalInput").ap()
    d_ones = nc.dram_tensor("ones_bf", [128, 128], BF16, kind="ExternalInput").ap()
    d_ident = nc.dram_tensor("ident_f32", [128, 128], F32, kind="ExternalInput").ap()
    d_eps = nc.dram_tensor("eps_col", [128, 2], F32, kind="ExternalInput").ap()

    d_y = nc.dram_tensor("y", [H, QT], F32, kind="ExternalOutput").ap()

    d_k_loc = nc.dram_tensor("k_loc", [H, QT], BF16)
    d_v_loc = nc.dram_tensor("v_loc", [H, QT], BF16)
    d_k_full = nc.dram_tensor("k_full", [NRANK * H, QT], BF16)
    d_v_full = nc.dram_tensor("v_full", [NRANK * H, QT], BF16)
    d_x2 = nc.dram_tensor("x2d", [H, QT], F32)
    d_x3p = nc.dram_tensor("x3p", [H, QT], F32)

    dbg = {}
    if stage < STAGE_FULL:
        dbg["xln"] = nc.dram_tensor("dbg_xln", [H, QT], BF16, kind="ExternalOutput").ap()
        dbg["q"] = nc.dram_tensor("dbg_q", [H, QT], BF16, kind="ExternalOutput").ap()
        dbg["kv"] = nc.dram_tensor("dbg_kv", [2 * H, QT], BF16, kind="ExternalOutput").ap()
        dbg["ctx"] = nc.dram_tensor("dbg_ctx", [H, QT], BF16, kind="ExternalOutput").ap()
        dbg["x2"] = nc.dram_tensor("dbg_x2", [H, QT], F32, kind="ExternalOutput").ap()
        dbg["xln2"] = nc.dram_tensor("dbg_xln2", [H, QT], BF16, kind="ExternalOutput").ap()

    cnt = {}
    sems = {}
    _SUB = int(os.environ.get("K_SUBSTAGE", "99"))
    _ASUB = int(os.environ.get("K_ASUB", "99"))

    def dma(h, out, in_):
        return h.dma_start(out=out, in_=in_)

    with ExitStack() as top:
        e = top.enter_context
        for s in SEM_NAMES:
            sems[s] = e(nc.semaphore(s))

        c_qkvb = e(nc.sbuf_tensor("c_qkvb", [128, CC_QKV], F32))
        c_ob = e(nc.sbuf_tensor("c_ob", [128, FC], F32))
        c_gs = e(nc.sbuf_tensor("c_gs", [128, RC], F32))
        c_us = e(nc.sbuf_tensor("c_us", [128, RC], F32))
        c_ds = e(nc.sbuf_tensor("c_ds", [128, FC], F32))
        c_ones = e(nc.sbuf_tensor("c_ones", [128, 128], BF16))
        c_ident = e(nc.sbuf_tensor("c_ident", [128, 128], F32))
        c_eps = e(nc.sbuf_tensor("c_eps", [128, 2], F32))
        t_mhq = e(nc.sbuf_tensor("t_mhq", [128, HEADS], F32))
        t_negm = e(nc.sbuf_tensor("t_negm", [128, 2], F32))
        t_sc1 = e(nc.sbuf_tensor("t_sc1", [128, 2], F32))

        sch = Sch(cnt)
        for dst, src in ((c_qkvb, d_qkvb), (c_ob, d_ob), (c_gs, d_gs), (c_us, d_us),
                         (c_ds, d_ds), (c_ones, d_ones), (c_ident, d_ident), (c_eps, d_eps)):
            sch.emit("sync", partial(lambda h, d_, s_: dma(h, d_[:], s_), d_=dst, s_=src),
                     inc="ld", amt=16)
        run_phase(nc, sems, sch)

        gstage = e(nc.sbuf_tensor("gstage", [128, 2 * FC * 128], BF16))
        ustage = e(nc.sbuf_tensor("ustage", [128, 2 * FC * 128], BF16))
        qctx_cm = nc.sbuf_tensor("qctx", [128, HEADS * QT], BF16)
        qctx = qctx_cm.__enter__()
        qctx_open = [True]

        def close_qctx():
            if qctx_open[0]:
                qctx_open[0] = False
                qctx_cm.__exit__(None, None, None)
        try:

                with ExitStack() as ctxB:
                    eB = ctxB.enter_context
                    xln = eB(nc.sbuf_tensor("xln", [128, FC * QT], BF16))
                    xc = eB(nc.sbuf_tensor("xc", [128, 2 * QT], F32))
                    xsq = eB(nc.sbuf_tensor("xsq", [128, QT], BF16))
                    tmp = eB(nc.sbuf_tensor("tmpf", [128, QT], F32))
                    wbf = eB(nc.sbuf_tensor("wbf", [128, 3 * 4096], BF16))
                    kevac = eB(nc.sbuf_tensor("kevac", [128, 2 * QT], BF16))

                    pSum1 = eB(nc.psum_tensor("pSum1", [128, QT], F32))
                    pSum2 = eB(nc.psum_tensor("pSum2", [128, QT], F32))
                    pA = eB(nc.psum_tensor("pA", [128, QT], F32))
                    pB = eB(nc.psum_tensor("pB", [128, QT], F32))

                    sch = Sch(cnt)
                    bw = list(bwaits(dict(cnt)))
                    cp_done, sq_done, st_pe = {}, {}, {}
                    for fc in range(FC):
                        sl = fc % 2
                        lx = f"lx{sl}"
                        buf = xc[:, sl * QT:sl * QT + QT]
                        w = list(bw) if fc == 0 else []
                        if fc >= 2:
                            w += [("dve", cp_done[fc - 2]), ("act", sq_done[fc - 2])]
                        sch.emit("sync", partial(lambda h, b, f: dma(h, b, d_x[f * 128:(f + 1) * 128, :]),
                                                 b=buf, f=fc), waits=w, inc=lx, amt=16)
                        ldv = sch.at(lx)
                        w = [(lx, ldv)] + (bw if fc == 0 else [])
                        sch.emit("vector", partial(
                            lambda h, b, f: h.tensor_copy(xln[:, f * QT:(f + 1) * QT], b),
                            b=buf, f=fc), waits=w, inc="dve")
                        cp_done[fc] = sch.at("dve")
                        w2 = list(w)
                        if fc >= 1:
                            w2.append(("pe", st_pe[fc - 1]))
                        sch.emit("scalar", partial(lambda h, b: h.activation(xsq[:], b, AF.Square), b=buf),
                                 waits=w2, inc="act")
                        sq_done[fc] = sch.at("act")
                        for qh in range(2):
                            sch.emit("tensor", partial(
                                lambda h, q, f: h.matmul(pSum1[:, q * 512:(q + 1) * 512], c_ones[:],
                                                         xln[:, f * QT + q * 512:f * QT + q * 512 + 512],
                                                         start=(f == 0), stop=(f == FC - 1)),
                                q=qh, f=fc),
                                waits=([("dve", cp_done[fc])] + (bw if fc == 0 else [])))
                            sch.emit("tensor", partial(
                                lambda h, q, f: h.matmul(pSum2[:, q * 512:(q + 1) * 512], c_ones[:],
                                                         xsq[:, q * 512:(q + 1) * 512],
                                                         start=(f == 0), stop=(f == FC - 1)),
                                q=qh, f=fc),
                                waits=[("act", sq_done[fc])],
                                inc="pe" if qh == 1 else None)
                        st_pe[fc] = sch.at("pe")
                    pe_stats = sch.at("pe")
                    if _SUB < 2:
                        run_phase(nc, sems, sch)
                        _finish(nc, sems, cnt)
                        return nc, dbg
                    sch.emit("vector", lambda h: h.tensor_scalar(tmp[:], pSum1[:], 1.0 / H, None, OP.mult),
                             waits=[("pe", pe_stats)], inc="dve")
                    sch.emit("vector", lambda h: h.tensor_scalar(xc[:, 0:QT], pSum2[:], 1.0 / H, None,
                                                                 OP.mult), inc="dve")
                    sch.emit("vector", lambda h: h.tensor_tensor(pA[:], tmp[:], tmp[:], OP.mult),
                             inc="dve")
                    sch.emit("vector", lambda h: h.tensor_tensor(pB[:], xc[:, 0:QT], pA[:], OP.subtract),
                             inc="dve")
                    var_done = sch.at("dve")
                    sch.emit("scalar", lambda h: h.activation(tmp[:], pB[:], AF.Ln, bias=c_eps[:, 0:1]),
                             waits=[("dve", var_done)], inc="act")
                    sch.emit("scalar", lambda h: h.activation(pA[:], tmp[:], AF.Exp, scale=-0.5), inc="act")
                    a_done = sch.at("act")
                    sch.emit("vector", lambda h: h.tensor_scalar(tmp[:], pSum1[:], 1.0 / H, None, OP.mult),
                             waits=[("act", a_done)], inc="dve")
                    sch.emit("vector", lambda h: h.scalar_tensor_tensor(pB[:], tmp[:], -1.0, pA[:],
                                                                       OP.mult, OP.mult), inc="dve")
                    b_done = sch.at("dve")

                    for fc in range(FC):
                        sch.emit("vector", partial(
                            lambda h, f: h.tensor_tensor(tmp[:], xln[:, f * QT:(f + 1) * QT],
                                                         pA[:], OP.mult), f=fc),
                            waits=[("dve", b_done)], inc="dve")
                        sch.emit("vector", partial(lambda h, f: h.tensor_tensor(
                            xln[:, f * QT:(f + 1) * QT], tmp[:], pB[:], OP.add), f=fc), inc="dve")
                    xln_done = sch.at("dve")
                    if _SUB < 3:
                        run_phase(nc, sems, sch)
                        _finish(nc, sems, cnt)
                        return nc, dbg

                    pO = [pSum1, pSum2]
                    col_order = (list(range(32, 96)) + list(range(32)))
                    if _SUB == 3:
                        col_order = col_order[:4]
                    wmm_done, ev_dve, kv_st = {}, {}, {}
                    cvt_done = {}
                    kv_done_w = []

                    def emit_wload(i):
                        c = col_order[i]
                        sl = i % 3
                        lw = f"lw{sl}"
                        w = []
                        if i >= 3:
                            w.append(("pe", wmm_done.get(sl, 0)))
                        sch.emit("gpsimd", partial(
                            lambda h, s, c2: dma(
                                h, wbf[:, s * 4096:(s + 1) * 4096],
                                d_qkvw[c2].rearrange("p a b -> p (a b)")),
                            s=sl, c2=c), waits=w, inc=lw, amt=16)
                        cvt_done[i] = (lw, sch.at(lw))

                    emit_wload(0)
                    emit_wload(1)
                    for idx, c in enumerate(col_order):
                        sl = idx % 2
                        wsl = idx % 3
                        po = pO[sl]
                        if idx + 2 < len(col_order):
                            emit_wload(idx + 2)
                        w = [cvt_done[idx]]
                        if idx < 2:
                            w.append(("dve", xln_done))
                        if idx >= 2:
                            w.append(("dve", ev_dve.get(idx - 2, 0)))
                        for fc in range(FC):
                            for qh in range(2):
                                last = (fc == FC - 1 and qh == 1)
                                sch.emit("tensor", partial(
                                    lambda h, p_, s, f, q, first, lst: h.matmul(
                                        p_[:, q * 512:(q + 1) * 512],
                                        wbf[:, s * 4096 + f * 128:s * 4096 + f * 128 + 128],
                                        xln[:, f * QT + q * 512:f * QT + q * 512 + 512],
                                        start=first, stop=lst),
                                    p_=po, s=wsl, f=fc, q=qh, first=(fc == 0), lst=(fc == FC - 1)),
                                    waits=(w if fc == 0 and qh == 0 else ()),
                                    inc="pe" if last else None)
                        wmm_done[wsl] = sch.at("pe")
                        pev = sch.at("pe")
                        if c < 32:
                            sch.emit("vector", partial(
                                lambda h, c2, p_: h.tensor_scalar(
                                    qctx[:, c2 * QT:(c2 + 1) * QT], p_[:], c_qkvb[:, c2:c2 + 1], None, OP.add),
                                c2=c, p_=po), waits=[("pe", pev)], inc="dve")
                            ev_dve[idx] = sch.at("dve")
                        else:
                            lk = f"lk{sl}"
                            w3 = [("pe", pev)]
                            if idx >= 2:
                                w3.append((lk, kv_st.get(sl, 0)))
                            sch.emit("vector", partial(
                                lambda h, c2, p_, s: h.tensor_scalar(
                                    kevac[:, s * QT:(s + 1) * QT], p_[:], c_qkvb[:, c2:c2 + 1], None, OP.add),
                                c2=c, p_=po, s=sl), waits=w3, inc="dve")
                            ev_dve[idx] = sch.at("dve")
                            dloc = d_k_loc if c < 64 else d_v_loc
                            roff = (c - 32) * 128 if c < 64 else (c - 64) * 128
                            sch.emit("sync", partial(
                                lambda h, r2, s, dl: dma(h, dl.ap()[r2:r2 + 128, :],
                                                         kevac[:, s * QT:(s + 1) * QT]),
                                r2=roff, s=sl, dl=dloc), waits=[("dve", ev_dve[idx])], inc=lk, amt=16)
                            kv_st[sl] = sch.at(lk)
                        if idx == 31 and stage >= 2:
                            sch.emit("gpsimd", lambda h: h.collective_compute(
                                "AllGather", OP.bypass, replica_groups=[[0, 1, 2, 3], [4, 5, 6, 7]],
                                ins=[d_k_loc.ap().opt()], outs=[d_k_full.ap().opt()]),
                                waits=[("lk0", sch.at("lk0")), ("lk1", sch.at("lk1"))] + bw, inc="cc")
                        if idx == 63:
                            kv_done_w = [("lk0", sch.at("lk0")), ("lk1", sch.at("lk1"))]
                            if stage >= 2:
                                sch.emit("gpsimd", lambda h: h.collective_compute(
                                    "AllGather", OP.bypass, replica_groups=[[0, 1, 2, 3], [4, 5, 6, 7]],
                                    ins=[d_v_loc.ap().opt()], outs=[d_v_full.ap().opt()]),
                                    waits=kv_done_w, inc="cc")
                    q_done = sch.at("dve")
                    if _SUB < 5:
                        run_phase(nc, sems, sch)
                        _finish(nc, sems, cnt)
                        return nc, dbg

                    if stage < STAGE_FULL:
                        for fc in range(FC):
                            sch.emit("sync", partial(lambda h, f: dma(
                                h, dbg["xln"][f * 128:(f + 1) * 128, :], xln[:, f * QT:(f + 1) * QT]),
                                f=fc), waits=[("dve", xln_done)], inc="ld", amt=16)
                        for c in range(32):
                            sch.emit("sync", partial(lambda h, c2: dma(
                                h, dbg["q"][c2 * 128:(c2 + 1) * 128, :], qctx[:, c2 * QT:(c2 + 1) * QT]),
                                c2=c), waits=[("dve", q_done)], inc="ld", amt=16)
                        sch.emit("sync", lambda h: dma(h, dbg["kv"][0:H, :], d_k_loc.ap()),
                                 waits=list(kv_done_w), inc="ld", amt=16)
                        sch.emit("sync", lambda h: dma(h, dbg["kv"][H:2 * H, :], d_v_loc.ap()),
                                 waits=list(kv_done_w), inc="ld", amt=16)
                    run_phase(nc, sems, sch)

                if stage < 2:
                    _finish(nc, sems, cnt)
                    return nc, dbg

                with ExitStack() as ctxC:
                    eC = ctxC.enter_context
                    kbuf = eC(nc.sbuf_tensor("kbuf", [128, 2 * SEQ], BF16))
                    vbbuf = eC(nc.sbuf_tensor("vbbuf", [128, 2 * SEQ], BF16))
                    vfbuf = eC(nc.sbuf_tensor("vfbuf", [128, SEQ], F32))
                    vtm = eC(nc.sbuf_tensor("vtm", [128, 2 * SEQ], BF16))
                    ptb = eC(nc.sbuf_tensor("ptb", [128, 2 * QT], BF16))
                    rb = eC(nc.sbuf_tensor("rb", [128, QT], F32))
                    sqb = eC(nc.sbuf_tensor("sqb", [128, SEQ], BF16))
                    att_tmp = eC(nc.sbuf_tensor("att_tmp", [128, QT], F32))

                    pS = [eC(nc.psum_tensor("pS0", [128, QT], F32)),
                          eC(nc.psum_tensor("pS1", [128, QT], F32))]
                    pC = eC(nc.psum_tensor("pCc", [128, QT], F32))
                    pD = eC(nc.psum_tensor("pDd", [128, QT], F32))

                    sch = Sch(cnt)
                    bw = list(bwaits(dict(cnt)))

                    qn_pe = 0
                    for hh in range(HEADS):
                        sch.emit("vector", partial(
                            lambda h, h2: h.tensor_tensor(sqb[:, 0:QT], qctx[:, h2 * QT:(h2 + 1) * QT],
                                                          qctx[:, h2 * QT:(h2 + 1) * QT], OP.mult), h2=hh),
                            waits=(bw if hh == 0 else [("pe", qn_pe)]), inc="dve")
                        sq_d = sch.at("dve")
                        for qh in range(2):
                            sch.emit("tensor", partial(
                                lambda h, q: h.matmul(pS[0][:, q * 512:(q + 1) * 512], c_ones[:],
                                                      sqb[:, q * 512:(q + 1) * 512], start=True, stop=True),
                                q=qh), waits=([("dve", sq_d)] + (bw if hh == 0 else [])),
                                inc="pe" if qh == 1 else None)
                        qn_pe = sch.at("pe")
                        sch.emit("vector", partial(
                            lambda h, h2: h.tensor_reduce(t_mhq[:, h2:h2 + 1], pS[0][:], AXX, OP.max),
                            h2=hh), waits=[("pe", qn_pe)], inc="dve")
                    qnorm_dve = sch.at("dve")
                    if _ASUB < 2:
                        run_phase(nc, sems, sch)
                        _finish(nc, sems, cnt)
                        return nc, dbg

                    ccv = sch.at("cc")
                    kv_ld, prep_done, negm_done = {}, {}, {}
                    head_pe_all, ctx_evac, sums_read = {}, {}, {}

                    def emit_kv_load(hh):
                        slot = hh % 2
                        lv = f"lv{slot}"
                        w = [("cc", 1)]
                        if hh < 2:
                            w += bw
                        else:
                            w.append(("pe", head_pe_all[hh - 2]))
                            w.append(("dve", prep_done[hh - 2]))
                        for r in range(NRANK):
                            sch.emit("sync", partial(
                                lambda h, h2, r2, s2: dma(
                                    h, kbuf[:, s2 * SEQ + r2 * QT: s2 * SEQ + (r2 + 1) * QT],
                                    d_k_full.ap()[r2 * H + h2 * 128: r2 * H + h2 * 128 + 128, :]),
                                h2=hh, r2=r, s2=slot), waits=(w if r == 0 else ()), inc=lv, amt=16)
                        sch.emit("sync", fn=None, waits=[("cc", ccv)])
                        for r in range(NRANK):
                            sch.emit("sync", partial(
                                lambda h, h2, r2, s2: dma(
                                    h, vbbuf[:, s2 * SEQ + r2 * QT: s2 * SEQ + (r2 + 1) * QT],
                                    d_v_full.ap()[r2 * H + h2 * 128: r2 * H + h2 * 128 + 128, :]),
                                h2=hh, r2=r, s2=slot), inc=lv, amt=16)
                        kv_ld[hh] = (lv, sch.at(lv))

                    def emit_prep(hh, act_gate=None):
                        slot = hh % 2
                        sch.emit("vector", partial(
                            lambda h, s2: h.tensor_copy(vfbuf[:, 0:2048], vbbuf[:, s2 * SEQ:s2 * SEQ + 2048]),
                            s2=slot), waits=[kv_ld[hh]], inc="dve")
                        sch.emit("vector", partial(
                            lambda h, s2: h.tensor_copy(vfbuf[:, 2048:4096],
                                                        vbbuf[:, s2 * SEQ + 2048:s2 * SEQ + 4096]),
                            s2=slot), inc="dve")
                        vf_d = sch.at("dve")
                        evac_d = {}
                        for grp in range(4):
                            ps = pS[grp % 2]
                            w = [("dve", vf_d)]
                            if grp >= 2:
                                w.append(("dve", evac_d[grp - 2]))
                            if act_gate is not None:
                                w.append(act_gate)
                            for j in range(8):
                                kc = grp * 8 + j
                                sch.emit("tensor", partial(
                                    lambda h, ps_, j2, kc2: h.transpose(
                                        ps_[:, j2 * 128:(j2 + 1) * 128],
                                        vfbuf[:, kc2 * 128:(kc2 + 1) * 128], c_ident[:]),
                                    ps_=ps, j2=j, kc2=kc),
                                    waits=(w if j == 0 else ()), inc="pe" if j == 7 else None)
                            pe_v = sch.at("pe")
                            sch.emit("vector", partial(
                                lambda h, ps_, s2, g2: h.tensor_copy(
                                    vtm[:, s2 * SEQ + g2 * QT: s2 * SEQ + (g2 + 1) * QT], ps_[:]),
                                ps_=ps, s2=slot, g2=grp), waits=[("pe", pe_v)], inc="dve")
                            evac_d[grp] = sch.at("dve")
                        sch.emit("vector", partial(
                            lambda h, s2: h.tensor_tensor(sqb[:], kbuf[:, s2 * SEQ:(s2 + 1) * SEQ],
                                                          kbuf[:, s2 * SEQ:(s2 + 1) * SEQ], OP.mult),
                            s2=slot), inc="dve")
                        ksq_d = sch.at("dve")
                        for rnd in range(2):
                            w0 = [("dve", ksq_d)] if rnd == 0 else [("dve", sch.at("dve"))]
                            for j in range(4):
                                sch.emit("tensor", partial(
                                    lambda h, ps_, j2, r2: h.matmul(
                                        ps_[:, (j2 % 2) * 512:(j2 % 2) * 512 + 512], c_ones[:],
                                        sqb[:, r2 * 2048 + j2 * 512: r2 * 2048 + (j2 + 1) * 512],
                                        start=True, stop=True),
                                    ps_=pS[j // 2], j2=j, r2=rnd),
                                    waits=(w0 if j == 0 else ()), inc="pe" if j == 3 else None)
                            pe_v = sch.at("pe")
                            for half in range(2):
                                first = (rnd == 0 and half == 0)
                                sch.emit("vector", partial(
                                    lambda h, ps_, c2: h.tensor_reduce(t_sc1[:, c2:c2 + 1], ps_[:],
                                                                       AXX, OP.max),
                                    ps_=pS[half], c2=(0 if first else 1)),
                                    waits=[("pe", pe_v)], inc="dve")
                                if not first:
                                    sch.emit("vector", lambda h: h.tensor_tensor(
                                        t_sc1[:, 0:1], t_sc1[:, 0:1], t_sc1[:, 1:2], OP.max), inc="dve")
                        sch.emit("vector", partial(
                            lambda h, h2: h.tensor_tensor(t_sc1[:, 1:2], t_sc1[:, 0:1],
                                                          t_mhq[:, h2:h2 + 1], OP.mult), h2=hh),
                            waits=[("dve", qnorm_dve)], inc="dve")
                        mm_d = sch.at("dve")
                        sch.emit("scalar", lambda h: h.activation(t_sc1[:, 0:1], t_sc1[:, 1:2], AF.Ln),
                                 waits=[("dve", mm_d)], inc="act")
                        sch.emit("scalar", lambda h: h.activation(t_sc1[:, 1:2], t_sc1[:, 0:1], AF.Exp,
                                                                  scale=0.5), inc="act")
                        a_v = sch.at("act")
                        sch.emit("vector", partial(
                            lambda h, s2: h.tensor_scalar(t_negm[:, s2:s2 + 1], t_sc1[:, 1:2],
                                                          -SCALE * 1.05, None, OP.mult), s2=slot),
                            waits=[("act", a_v)], inc="dve")
                        negm_done[hh] = sch.at("dve")
                        prep_done[hh] = sch.at("dve")

                    emit_kv_load(0)
                    emit_kv_load(1)
                    emit_prep(0)
                    if _ASUB < 3:
                        run_phase(nc, sems, sch)
                        _finish(nc, sems, cnt)
                        return nc, dbg
                    _NH = HEADS if _ASUB >= 5 else {3: 1, 35: 2, 4: 4}.get(_ASUB, 4)

                    for hh in range(_NH):
                        slot = hh % 2
                        exp_done, ctx_pe = {}, {}

                        def emit_ctx_sums(kc, _hh=None, _slot=None):
                            _hh = hh if _hh is None else _hh
                            _slot = slot if _slot is None else _slot
                            w = [("act", exp_done[kc])]
                            if kc == 0:
                                w.append(("dve", ctx_evac.get(_hh - 1, 0)))
                                w.append(("act", sums_read.get(_hh - 1, 0)))
                            for qh in range(2):
                                sch.emit("tensor", partial(
                                    lambda h, s2, kc2, q: h.matmul(
                                        pC[:, q * 512:(q + 1) * 512],
                                        vtm[:, s2 * SEQ + kc2 * 128:s2 * SEQ + (kc2 + 1) * 128],
                                        ptb[:, (kc2 % 2) * QT + q * 512:(kc2 % 2) * QT + q * 512 + 512],
                                        start=(kc2 == 0), stop=(kc2 == KC - 1)),
                                    s2=_slot, kc2=kc, q=qh), waits=(w if qh == 0 else ()))
                                sch.emit("tensor", partial(
                                    lambda h, kc2, q: h.matmul(
                                        pD[:, q * 512:(q + 1) * 512], c_ones[:],
                                        ptb[:, (kc2 % 2) * QT + q * 512:(kc2 % 2) * QT + q * 512 + 512],
                                        start=(kc2 == 0), stop=(kc2 == KC - 1)),
                                    kc2=kc, q=qh), inc="pe" if qh == 1 else None)
                            ctx_pe[kc] = sch.at("pe")

                        for kc in range(KC):
                            ps = pS[kc % 2]
                            w = []
                            if kc < 2:
                                w.append(("dve", prep_done[hh]))
                            else:
                                w.append(("act", exp_done[kc - 2]))
                            for qh in range(2):
                                sch.emit("tensor", partial(
                                    lambda h, ps_, s2, kc2, q, h2: h.matmul(
                                        ps_[:, q * 512:(q + 1) * 512],
                                        kbuf[:, s2 * SEQ + kc2 * 128: s2 * SEQ + (kc2 + 1) * 128],
                                        qctx[:, h2 * QT + q * 512: h2 * QT + q * 512 + 512],
                                        start=True, stop=True),
                                    ps_=ps, s2=slot, kc2=kc, q=qh, h2=hh),
                                    waits=(w if qh == 0 else ()), inc="pe" if qh == 1 else None)
                            s_pe = sch.at("pe")
                            w = [("pe", s_pe), ("dve", negm_done[hh])]
                            if kc >= 2:
                                w.append(("pe", ctx_pe[kc - 2]))
                            sch.emit("scalar", partial(
                                lambda h, ps_, kc2, s2: h.activation(
                                    ptb[:, (kc2 % 2) * QT:(kc2 % 2) * QT + QT], ps_[:], AF.Exp,
                                    bias=t_negm[:, s2:s2 + 1], scale=SCALE),
                                ps_=ps, kc2=kc, s2=slot), waits=w, inc="act")
                            exp_done[kc] = sch.at("act")
                            if kc >= 1:
                                emit_ctx_sums(kc - 1)
                        emit_ctx_sums(KC - 1)
                        head_pe_all[hh] = sch.at("pe")
                        if hh + 2 < _NH:
                            emit_kv_load(hh + 2)
                        if hh + 1 < _NH:
                            emit_prep(hh + 1, act_gate=("act", exp_done[KC - 1]))
                        sch.emit("scalar", lambda h: h.activation(att_tmp[:], pD[:], AF.Ln, bias=c_eps[:, 1:2]),
                                 waits=[("pe", head_pe_all[hh])], inc="act")
                        sums_read[hh] = sch.at("act")
                        sch.emit("scalar", lambda h: h.activation(rb[:], att_tmp[:], AF.Exp, scale=-1.0),
                                 waits=[("dve", ctx_evac.get(hh - 1, 0))], inc="act")
                        rb_done = sch.at("act")
                        sch.emit("vector", partial(
                            lambda h, h2: h.tensor_tensor(qctx[:, h2 * QT:(h2 + 1) * QT], pC[:], rb[:],
                                                          OP.mult), h2=hh),
                            waits=[("act", rb_done)], inc="dve")
                        ctx_evac[hh] = sch.at("dve")
                    ctx_done = sch.at("dve")

                    if stage < STAGE_FULL:
                        for c in range(32):
                            sch.emit("sync", partial(lambda h, c2: dma(
                                h, dbg["ctx"][c2 * 128:(c2 + 1) * 128, :], qctx[:, c2 * QT:(c2 + 1) * QT]),
                                c2=c), waits=[("dve", ctx_done)], inc="ld", amt=16)
                    run_phase(nc, sems, sch)

                if stage < 3:
                    _finish(nc, sems, cnt)
                    return nc, dbg

                ctxD = ExitStack()
                with ExitStack() as ctxP:
                    eP = ctxP.enter_context
                    eD = ctxD.enter_context
                    wbf = eD(nc.sbuf_tensor("wbf2", [128, 3 * 4096], BF16))
                    xc = eD(nc.sbuf_tensor("xc2", [128, 2 * QT], F32))
                    x2b = eD(nc.sbuf_tensor("x2b", [128, 2 * QT], F32))
                    x2bf = eD(nc.sbuf_tensor("x2bf", [128, QT], BF16))
                    x2sq = eD(nc.sbuf_tensor("x2sq", [128, QT], BF16))
                    tmp = eD(nc.sbuf_tensor("tmpf2", [128, QT], F32))

                    pO = [eP(nc.psum_tensor("pO0", [128, QT], F32)),
                          eP(nc.psum_tensor("pO1", [128, QT], F32))]
                    pSum1 = eP(nc.psum_tensor("pSum1b", [128, QT], F32))
                    pSum2 = eP(nc.psum_tensor("pSum2b", [128, QT], F32))

                    sch = Sch(cnt)
                    bw = list(bwaits(dict(cnt)))
                    p5_bw = list(bw)
                    wmm_done, ev_dve, x2_st = {}, {}, {}
                    cp_d, sq_d, stats_pe = {}, {}, {}

                    def emit_stats(c):
                        for qh in range(2):
                            sch.emit("tensor", partial(
                                lambda h, q, c2: h.matmul(
                                    pSum1[:, q * 512:(q + 1) * 512], c_ones[:],
                                    x2bf[:, q * 512:(q + 1) * 512], start=(c2 == 0), stop=(c2 == FC - 1)),
                                q=qh, c2=c), waits=([("dve", cp_d[c])] if qh == 0 else ()))
                            sch.emit("tensor", partial(
                                lambda h, q, c2: h.matmul(
                                    pSum2[:, q * 512:(q + 1) * 512], c_ones[:],
                                    x2sq[:, q * 512:(q + 1) * 512], start=(c2 == 0), stop=(c2 == FC - 1)),
                                q=qh, c2=c), waits=([("act", sq_d[c])] if qh == 0 else ()),
                                inc="pe" if qh == 1 else None)
                        stats_pe[c] = sch.at("pe")

                    cvt_done = {}

                    def emit_wload(i):
                        sl = i % 3
                        lw = f"lw{sl}"
                        w = list(bw) if i < 3 else [("pe", wmm_done.get(sl, 0))]
                        sch.emit("gpsimd", partial(
                            lambda h, s, c2: dma(
                                h, wbf[:, s * 4096:(s + 1) * 4096],
                                d_ow[c2].rearrange("p a b -> p (a b)")),
                            s=sl, c2=i), waits=w, inc=lw, amt=16)
                        cvt_done[i] = (lw, sch.at(lw))

                    emit_wload(0)
                    emit_wload(1)
                    for c in range(FC):
                        sl = c % 2
                        wsl = c % 3
                        po = pO[sl]
                        lx = f"lx{sl}"
                        w = list(bw) if c == 0 else ([("dve", ev_dve[c - 2])] if c >= 2 else [])
                        sch.emit("sync", partial(
                            lambda h, c2, s: dma(h, xc[:, s * QT:(s + 1) * QT],
                                                 d_x[c2 * 128:(c2 + 1) * 128, :]),
                            c2=c, s=sl), waits=w, inc=lx, amt=16)
                        x1_ld = sch.at(lx)
                        if c + 2 < FC:
                            emit_wload(c + 2)
                        w = [cvt_done[c]]
                        if c >= 2:
                            w.append(("dve", ev_dve[c - 2]))
                        for fc in range(FC):
                            for qh in range(2):
                                last = (fc == FC - 1 and qh == 1)
                                sch.emit("tensor", partial(
                                    lambda h, p_, s, f, q, first, lst: h.matmul(
                                        p_[:, q * 512:(q + 1) * 512],
                                        wbf[:, s * 4096 + f * 128:s * 4096 + f * 128 + 128],
                                        qctx[:, f * QT + q * 512:f * QT + q * 512 + 512],
                                        start=first, stop=lst),
                                    p_=po, s=wsl, f=fc, q=qh, first=(fc == 0), lst=(fc == FC - 1)),
                                    waits=(w if fc == 0 and qh == 0 else ()),
                                    inc="pe" if last else None)
                        wmm_done[wsl] = sch.at("pe")
                        pev = sch.at("pe")
                        if c >= 1:
                            emit_stats(c - 1)
                        ls = f"ls{sl}"
                        w = [("pe", pev), (lx, x1_ld)]
                        if c >= 2:
                            w.append((ls, x2_st.get(sl, 0)))
                        sch.emit("vector", partial(
                            lambda h, c2, s, p_: h.scalar_tensor_tensor(
                                x2b[:, s * QT:(s + 1) * QT], p_[:], c_ob[:, c2:c2 + 1],
                                xc[:, s * QT:(s + 1) * QT], OP.add, OP.add),
                            c2=c, s=sl, p_=po), waits=w, inc="dve")
                        x2_d = sch.at("dve")
                        sch.emit("sync", partial(
                            lambda h, c2, s: dma(h, d_x2.ap()[c2 * 128:(c2 + 1) * 128, :],
                                                 x2b[:, s * QT:(s + 1) * QT]),
                            c2=c, s=sl), waits=[("dve", x2_d)], inc=ls, amt=16)
                        x2_st[sl] = sch.at(ls)
                        w = []
                        if c >= 1:
                            w.append(("pe", stats_pe[c - 1]))
                        sch.emit("vector", partial(
                            lambda h, s: h.tensor_copy(x2bf[:], x2b[:, s * QT:(s + 1) * QT]), s=sl),
                            waits=w, inc="dve")
                        cp_d[c] = sch.at("dve")
                        sch.emit("scalar", partial(
                            lambda h, s: h.activation(x2sq[:], x2b[:, s * QT:(s + 1) * QT], AF.Square),
                            s=sl), waits=[("dve", x2_d)] + w, inc="act")
                        sq_d[c] = sch.at("act")
                        ev_dve[c] = sch.at("dve")
                    emit_stats(FC - 1)
                    x2_all_w = [("ls0", sch.at("ls0")), ("ls1", sch.at("ls1"))]
                    pe_stats = sch.at("pe")
                    sch.emit("vector", lambda h: h.tensor_scalar(tmp[:], pSum2[:], 1.0 / H, None, OP.mult),
                             waits=[("pe", pe_stats), ("dve", ev_dve[FC - 1])], inc="dve")
                    sch.emit("vector", lambda h: h.tensor_scalar(x2b[:, 0:QT], pSum1[:], 1.0 / H, None,
                                                                 OP.mult), inc="dve")
                    sch.emit("vector", lambda h: h.tensor_tensor(pO[0][:], x2b[:, 0:QT], x2b[:, 0:QT],
                                                                 OP.mult), inc="dve")
                    sch.emit("vector", lambda h: h.tensor_tensor(tmp[:], tmp[:], pO[0][:], OP.subtract),
                             inc="dve")
                    var_d = sch.at("dve")
                    sch.emit("scalar", lambda h: h.activation(x2b[:, QT:2 * QT], tmp[:], AF.Ln,
                                                              bias=c_eps[:, 0:1]),
                             waits=[("dve", var_d)], inc="act")
                    sch.emit("scalar", lambda h: h.activation(pO[0][:], x2b[:, QT:2 * QT], AF.Exp,
                                                              scale=-0.5), inc="act")
                    a2_act = sch.at("act")
                    sch.emit("vector", lambda h: h.scalar_tensor_tensor(pO[1][:], x2b[:, 0:QT], -1.0,
                                                                        pO[0][:], OP.mult, OP.mult),
                             waits=[("act", a2_act)], inc="dve")
                    b2_d = sch.at("dve")

                    if stage < STAGE_FULL:
                        sch.emit("sync", lambda h: dma(h, dbg["x2"], d_x2.ap()),
                                 waits=list(x2_all_w), inc="ld", amt=16)
                    run_phase(nc, sems, sch)
                    ctxD.close()

                    if stage < 4:
                        _finish(nc, sems, cnt)
                        return nc, dbg

                    with ExitStack() as ctxE:
                        eE = ctxE.enter_context
                        xln2 = qctx
                        dstage = eE(nc.sbuf_tensor("dstage", [128, 2 * 29 * 128], BF16))
                        hbuf = eE(nc.sbuf_tensor("hbuf", [128, 29 * QT], BF16))
                        sg = eE(nc.sbuf_tensor("sg", [128, QT], BF16))
                        fxc = eE(nc.sbuf_tensor("fxc", [128, 2 * QT], F32))
                        fyb = eE(nc.sbuf_tensor("fyb", [128, 2 * QT], F32))
                        fn1 = eE(nc.sbuf_tensor("fn1", [128, QT], F32))

                        sch = Sch(cnt)
                        dump_w = [("ld", sch.at("ld"))] if stage < STAGE_FULL else []
                        for fc in range(FC):
                            sch.emit("gpsimd", partial(
                                lambda h, f: dma(h, xln2[:, f * QT:(f + 1) * QT],
                                                 d_x2.ap()[f * 128:(f + 1) * 128, :]),
                                f=fc),
                                waits=([("pe", pe_stats)] + list(x2_all_w) + dump_w
                                       if fc == 0 else ()), inc="lk0", amt=16)
                        x2cast_done = sch.at("lk0")
                        norm_done = {}
                        for fc in range(FC):
                            w = [("dve", b2_d)]
                            if fc == 0:
                                w.append(("lk0", x2cast_done))
                            sch.emit("vector", partial(
                                lambda h, f: h.tensor_tensor(fn1[:], xln2[:, f * QT:(f + 1) * QT],
                                                             pO[0][:], OP.mult), f=fc),
                                waits=w, inc="dve")
                            sch.emit("vector", partial(
                                lambda h, f: h.tensor_tensor(xln2[:, f * QT:(f + 1) * QT], fn1[:],
                                                             pO[1][:], OP.add), f=fc), inc="dve")
                            norm_done[fc] = sch.at("dve")
                        xln2_done = sch.at("dve")

                        if stage < STAGE_FULL:
                            for fc in range(FC):
                                sch.emit("sync", partial(lambda h, f: dma(
                                    h, dbg["xln2"][f * 128:(f + 1) * 128, :], xln2[:, f * QT:(f + 1) * QT]),
                                    f=fc), waits=[("dve", xln2_done)], inc="ld", amt=16)

                        pG = [pSum1, pO[0]]
                        pU = [pSum2, pO[1]]
                        pDn = [pSum1, pSum2]
                        gu_slot_pe = {}
                        pg_slot_dve = {}
                        dn_pe_done, ev_dve2, y_st2 = {}, {}, {}
                        down_ev_tail = 0
                        last_h_done = 0
                        prev_pass_st_w = []
                        for pi, (rc0, rc1) in enumerate(FFN_PASSES):
                            nrc = rc1 - rc0
                            for j in range(nrc):
                                rc = rc0 + j
                                sl = j % 2
                                gg = f"gg{sl}"
                                w = [("pe", gu_slot_pe.get(sl, 0))]
                                for (stg, src) in ((gstage, d_gw), (ustage, d_uw)):
                                    sch.emit("gpsimd", partial(
                                        lambda h, st_, sr_, r2, s2: dma(
                                            h, st_[:, s2 * 4096:(s2 + 1) * 4096],
                                            sr_[r2].rearrange("p a b -> p (a b)")),
                                        st_=stg, sr_=src, r2=rc, s2=sl), waits=w, inc=gg, amt=16)
                                gu_ldv = sch.at(gg)
                                wmm = [(gg, gu_ldv), ("dve", pg_slot_dve.get(sl, 0))]
                                if pi == 0 and j < 2:
                                    wmm.append(("dve", xln2_done))
                                if j < 2 and pi > 0:
                                    wmm.append(("dve", down_ev_tail))
                                for (pp, stg) in ((pG, gstage), (pU, ustage)):
                                    for fc in range(FC):
                                        for qh in range(2):
                                            last = (fc == FC - 1 and qh == 1)
                                            sch.emit("tensor", partial(
                                                lambda h, pp_, st_, s, f, q, first, lst: h.matmul(
                                                    pp_[s][:, q * 512:(q + 1) * 512],
                                                    st_[:, s * 4096 + f * 128:s * 4096 + f * 128 + 128],
                                                    xln2[:, f * QT + q * 512:f * QT + q * 512 + 512],
                                                    start=first, stop=lst),
                                                pp_=pp, st_=stg, s=sl, f=fc, q=qh,
                                                first=(fc == 0), lst=(fc == FC - 1)),
                                                waits=(wmm if (fc == 0 and qh == 0 and stg is gstage) else ()),
                                                inc=("pe" if (last and stg is ustage) else None))
                                gu_slot_pe[sl] = sch.at("pe")
                                gu_pe = sch.at("pe")
                                sch.emit("scalar", partial(
                                    lambda h, s, r2: h.activation(sg[:], pG[s][:], AF.Silu,
                                                                  scale=c_gs[:, r2:r2 + 1]),
                                    s=sl, r2=rc),
                                    waits=[("pe", gu_pe), ("dve", last_h_done)], inc="act")
                                sg_dn = sch.at("act")
                                sch.emit("vector", partial(
                                    lambda h, s, r2, j2: h.scalar_tensor_tensor(
                                        hbuf[:, j2 * QT:(j2 + 1) * QT], pU[s][:], c_us[:, r2:r2 + 1],
                                        sg[:], OP.mult, OP.mult),
                                    s=sl, r2=rc, j2=j), waits=[("act", sg_dn)], inc="dve")
                                pg_slot_dve[sl] = sch.at("dve")
                                last_h_done = sch.at("dve")
                            h_all = sch.at("dve")
                            last_pass = (pi == len(FFN_PASSES) - 1)
                            src_res = d_x2 if pi == 0 else d_x3p
                            for c in range(FC):
                                dcc = pi * FC + c
                                sl = c % 2
                                gd = f"gd{sl}"
                                lx = f"lx{sl}"
                                ly = f"ly{sl}"
                                w = [("dve", h_all)] if c < 2 else [("pe", dn_pe_done.get(dcc - 2, 0))]
                                sch.emit("gpsimd", partial(
                                    lambda h, c2, s2, a2, b2_: dma(
                                        h, dstage[:, s2 * 29 * 128:s2 * 29 * 128 + (b2_ - a2) * 128],
                                        d_dw[c2, :, a2:b2_, :].rearrange("p a b -> p (a b)")),
                                    c2=c, s2=sl, a2=rc0, b2_=rc1), waits=w, inc=gd, amt=16)
                                d_ldv = sch.at(gd)
                                w = [("dve", ev_dve2.get(dcc - 2, xln2_done))] + \
                                    (prev_pass_st_w if c < 2 else [])
                                sch.emit("sync", partial(
                                    lambda h, c2, s2, sr_: dma(h, fxc[:, s2 * QT:(s2 + 1) * QT],
                                                               sr_.ap()[c2 * 128:(c2 + 1) * 128, :]),
                                    c2=c, s2=sl, sr_=src_res), waits=w, inc=lx, amt=16)
                                res_ld = sch.at(lx)
                                wmm = [(gd, d_ldv), ("dve", h_all)]
                                if c >= 2:
                                    wmm.append(("dve", ev_dve2.get(dcc - 2, 0)))
                                for j in range(nrc):
                                    for qh in range(2):
                                        last = (j == nrc - 1 and qh == 1)
                                        sch.emit("tensor", partial(
                                            lambda h, s, j2, q, first, lst: h.matmul(
                                                pDn[s][:, q * 512:(q + 1) * 512],
                                                dstage[:, s * 29 * 128 + j2 * 128:
                                                          s * 29 * 128 + j2 * 128 + 128],
                                                hbuf[:, j2 * QT + q * 512:j2 * QT + q * 512 + 512],
                                                start=first, stop=lst),
                                            s=sl, j2=j, q=qh, first=(j == 0), lst=(j == nrc - 1)),
                                            waits=(wmm if (j == 0 and qh == 0) else ()),
                                            inc="pe" if last else None)
                                dn_pe_done[dcc] = sch.at("pe")
                                w = [("pe", dn_pe_done[dcc]), (lx, res_ld)]
                                if y_st2.get(dcc - 2):
                                    w.append(y_st2[dcc - 2])
                                sch.emit("vector", partial(
                                    lambda h, c2, s: h.scalar_tensor_tensor(
                                        fyb[:, s * QT:(s + 1) * QT], pDn[s][:], c_ds[:, c2:c2 + 1],
                                        fxc[:, s * QT:(s + 1) * QT], OP.mult, OP.add),
                                    c2=c, s=sl), waits=w, inc="dve")
                                ev_dve2[dcc] = sch.at("dve")
                                dst = d_y if last_pass else d_x3p.ap()
                                sch.emit("sync", partial(
                                    lambda h, c2, s, dd: dma(h, dd[c2 * 128:(c2 + 1) * 128, :],
                                                             fyb[:, s * QT:(s + 1) * QT]),
                                    c2=c, s=sl, dd=dst), waits=[("dve", ev_dve2[dcc])], inc=ly, amt=16)
                                y_st2[dcc] = (ly, sch.at(ly))
                            down_ev_tail = sch.at("dve")
                            prev_pass_st_w = [("ly0", sch.at("ly0")), ("ly1", sch.at("ly1"))]
                        run_phase(nc, sems, sch)
        finally:
            close_qctx()

        _finish(nc, sems, cnt)
    return nc, dbg


def _finish(nc, sems, cnt):
    sch = Sch(cnt)
    sch.emit("sync", fn=None, waits=bwaits(dict(cnt)))
    run_phase(nc, sems, sch)



def _prep_shared(qkvw, qkvb, ow, ob, gw, gs, uw, us, dw, ds):
    f32 = np.float32
    return {
        "qkvw": np.ascontiguousarray(
            np.asarray(qkvw, f32).reshape(FC, 128, CC_QKV, 128).transpose(2, 1, 0, 3)),
        "qkvb": np.ascontiguousarray(np.asarray(qkvb, f32).reshape(CC_QKV, 128).T),
        "ow": np.ascontiguousarray(
            np.asarray(ow, f32).reshape(FC, 128, FC, 128).transpose(2, 1, 0, 3)),
        "ob": np.ascontiguousarray(np.asarray(ob, f32).reshape(FC, 128).T),
        "gw": np.ascontiguousarray(
            np.asarray(gw, np.int32).reshape(RC, 128, FC, 128).transpose(0, 3, 2, 1)),
        "uw": np.ascontiguousarray(
            np.asarray(uw, np.int32).reshape(RC, 128, FC, 128).transpose(0, 3, 2, 1)),
        "dw": np.ascontiguousarray(
            np.asarray(dw, np.int32).reshape(FC, 128, RC, 128).transpose(0, 3, 2, 1)),
        "gs": np.ascontiguousarray(np.asarray(gs, f32).reshape(RC, 128).T),
        "us": np.ascontiguousarray(np.asarray(us, f32).reshape(RC, 128).T),
        "ds": np.ascontiguousarray(np.asarray(ds, f32).reshape(FC, 128).T),
        "ones_bf": np.ones((128, 128), dtype=ml_dtypes.bfloat16),
        "ident_f32": np.eye(128, dtype=f32),
        "eps_col": np.concatenate([np.full((128, 1), EPS, dtype=f32),
                                   np.full((128, 1), 1e-38, dtype=f32)], axis=1),
    }


def _in_maps(x, shared):
    x = np.asarray(x, np.float32)
    maps = []
    for c in range(8):
        b, qr = c // 4, c % 4
        m = {"x": np.ascontiguousarray(x[b, qr * QT:(qr + 1) * QT, :].T)}
        m.update(shared)
        maps.append(m)
    return maps


def kernel(L0_attn_x, L0_attn_attn_qkvw, L0_attn_attn_qkvb, L0_attn_attn_ow,
           L0_attn_attn_ob, L0_ffn_gate_w, L0_ffn_gate_s, L0_ffn_up_w,
           L0_ffn_up_s, L0_ffn_down_w, L0_ffn_down_s):
    shared = _prep_shared(L0_attn_attn_qkvw, L0_attn_attn_qkvb, L0_attn_attn_ow,
                          L0_attn_attn_ob, L0_ffn_gate_w, L0_ffn_gate_s,
                          L0_ffn_up_w, L0_ffn_up_s, L0_ffn_down_w, L0_ffn_down_s)
    nc, _ = _build(STAGE_FULL)
    maps = _in_maps(L0_attn_x, shared)
    res = run_bass_kernel_spmd(nc, maps, core_ids=list(range(8)))
    out = np.empty((2, SEQ, H), dtype=np.float32)
    for c in range(8):
        b, qr = c // 4, c % 4
        out[b, qr * QT:(qr + 1) * QT, :] = np.asarray(res.results[c]["y"]).T
    return out
